# version 06
# baseline (speedup 1.0000x reference)
import math
import contextlib
import numpy as np
import ml_dtypes

import concourse.bass as bass
import concourse.mybir as mybir
from concourse.bass_utils import run_bass_kernel_spmd

F32 = mybir.dt.float32
BF16 = mybir.dt.bfloat16
I16 = mybir.dt.int16
ALU = mybir.AluOpType
ACT = mybir.ActivationFunctionType

D = 2048
T = 2048
NB = 2
NCH = 16
NH = 16
FF = 8192
NFC = 64
TGW = 512
NTG = NB * (T // TGW)
NSTEEP = 7
QREF = 292
INV_SQRT_HD = float(1.0 / math.sqrt(128.0))

_BF16 = ml_dtypes.bfloat16


def _split4(v):
    parts = []
    r = np.array(v, dtype=np.float64)
    for _ in range(4):
        p = r.astype(_BF16)
        parts.append(p)
        r = r - p.astype(np.float64)
    return parts


def _slopes64():
    h = np.arange(1, NH + 1, dtype=np.float64)
    return 0.5 ** (h * 8.0 / NH)


def build_consts():
    sl = _slopes64()
    A = math.sqrt(128.0) * sl
    C = np.maximum(0.0, sl * 127.0 - 75.0)

    onescol = np.ones((128, 1), dtype=_BF16)
    lnones = np.full((128, 1), 1.0 / 2048.0, dtype=_BF16)
    onesrow = np.ones((1, 128), dtype=_BF16)
    tri = (np.arange(128)[:, None] <= np.arange(128)[None, :]).astype(_BF16)

    j = np.arange(512, dtype=np.float64)
    brhs = np.zeros((12, 512), dtype=_BF16)
    brhs[0:4] = _BF16(1.0)
    for r in range(4, 8):
        brhs[r] = (j - 256.0).astype(_BF16)
    brhs[8:12] = _BF16(1.0)

    blhs = np.zeros((12, NSTEEP * 16 * 128), dtype=_BF16)
    i = np.arange(128, dtype=np.float64)
    for hh in range(NSTEEP):
        ai_p = _split4(A[hh] * i)
        mA_p = _split4(np.full(128, -A[hh]))
        for o in range(16):
            d128 = o - 12
            b0 = math.sqrt(128.0) * (sl[hh] * (128.0 * d128 - 256.0) - C[hh])
            b0_p = _split4(np.full(128, b0))
            c0 = (hh * 16 + o) * 128
            for r in range(4):
                blhs[r, c0:c0 + 128] = ai_p[r]
                blhs[4 + r, c0:c0 + 128] = mA_p[r]
                blhs[8 + r, c0:c0 + 128] = b0_p[r]

    ebias = np.zeros((128, (NH - NSTEEP) * 16), dtype=np.float32)
    for hh in range(NSTEEP, NH):
        for o in range(16):
            ebias[:, (hh - NSTEEP) * 16 + o] = (
                sl[hh] * (128.0 * (o - 12) + i - float(QREF))).astype(np.float32)

    return {
        "onescol": np.asarray(onescol),
        "lnones": np.asarray(lnones),
        "onesrow": np.asarray(onesrow),
        "trimask": np.asarray(tri),
        "biasrhs": np.asarray(brhs),
        "biaslhs": np.asarray(blhs),
        "expbias": ebias,
    }


_SCRATCH_KIND = "Internal"


def _dram_io(nc):
    sk = _SCRATCH_KIND
    H = {}
    H["xt"] = nc.dram_tensor("xt", [NB, D, T], F32, kind="ExternalInput").ap()
    for w in ("wq", "wk", "wv", "wo"):
        H[w] = nc.dram_tensor(w, [128, NCH, D], F32, kind="ExternalInput").ap()
    H["w1t"] = nc.dram_tensor("w1t", [NFC, 128, D], F32, kind="ExternalInput").ap()
    H["w2t"] = nc.dram_tensor("w2t", [4, 128, NFC * 512], F32, kind="ExternalInput").ap()
    H["onescol"] = nc.dram_tensor("onescol", [128, 1], BF16, kind="ExternalInput").ap()
    H["lnones"] = nc.dram_tensor("lnones", [128, 1], BF16, kind="ExternalInput").ap()
    H["onesrow"] = nc.dram_tensor("onesrow", [1, 128], BF16, kind="ExternalInput").ap()
    H["trimask"] = nc.dram_tensor("trimask", [128, 128], BF16, kind="ExternalInput").ap()
    H["biasrhs"] = nc.dram_tensor("biasrhs", [12, 512], BF16, kind="ExternalInput").ap()
    H["biaslhs"] = nc.dram_tensor("biaslhs", [12, NSTEEP * 16 * 128], BF16, kind="ExternalInput").ap()
    H["expbias"] = nc.dram_tensor("expbias", [128, (NH - NSTEEP) * 16], F32, kind="ExternalInput").ap()
    H["xn_h"] = nc.dram_tensor("xn_h", [NB, NCH, 128, T], BF16, kind=sk).ap()
    H["q_h"] = nc.dram_tensor("q_h", [NB, NCH, 128, T], BF16, kind=sk).ap()
    H["k_h"] = nc.dram_tensor("k_h", [NB, NCH, 128, T], BF16, kind=sk).ap()
    H["v_h"] = nc.dram_tensor("v_h", [NB, NCH, 128, T], BF16, kind=sk).ap()
    H["o_h"] = nc.dram_tensor("o_h", [NB, NCH, 128, T], BF16, kind=sk).ap()
    H["s_h"] = nc.dram_tensor("s_h", [NB, NCH, 128, T], BF16, kind=sk).ap()
    H["a_h"] = nc.dram_tensor("a_h", [NB * T // TGW, 128, NFC, TGW], BF16, kind=sk).ap()
    H["outt"] = nc.dram_tensor("outt", [NB, D, T], F32, kind="ExternalOutput").ap()
    return H


def phase_consts(nc, H, S):
    with contextlib.ExitStack() as _es:
        ld = _es.enter_context(nc.semaphore("c_ld"))
        blk = _es.enter_context(nc.Block("consts"))
        @blk.sync
        def _(sync):
            n = 0
            for k in ("onescol", "lnones", "onesrow"):
                sync.dma_start(S[k][:, :], H[k][:, :]).then_inc(ld, 16)
                n += 16
            sync.wait_ge(ld, n)

        @blk.vector
        def _(v):
            v.memset(S["eps"][:, :], 1e-5)


def phase_ln(nc, H, S):
    GN = NTG
    with contextlib.ExitStack() as _es:
        xg = _es.enter_context(nc.sbuf_tensor("ln_x", [128, 2 * NCH * TGW], F32))
        xbg = _es.enter_context(nc.sbuf_tensor("ln_xb", [128, NCH * TGW], BF16))
        x2g = _es.enter_context(nc.sbuf_tensor("ln_x2", [128, NCH * TGW], BF16))
        xng = _es.enter_context(nc.sbuf_tensor("ln_xn", [128, 2 * NCH * TGW], BF16))
        rows = _es.enter_context(nc.sbuf_tensor("ln_rows", [1, 6 * TGW], F32))
        rows16 = _es.enter_context(nc.sbuf_tensor("ln_rows16", [1, 2 * TGW], BF16))
        mps = _es.enter_context(nc.psum_tensor("ln_m", [1, TGW], F32))
        e2ps = _es.enter_context(nc.psum_tensor("ln_e2", [1, TGW], F32))
        Mps = _es.enter_context(nc.psum_tensor("ln_M", [128, TGW], F32))
        Rps = _es.enter_context(nc.psum_tensor("ln_R", [128, TGW], F32))
        xl0 = _es.enter_context(nc.semaphore("ln_xl0"))
        xl1 = _es.enter_context(nc.semaphore("ln_xl1"))
        vc = _es.enter_context(nc.semaphore("ln_vc"))
        ts = _es.enter_context(nc.semaphore("ln_ts"))
        st1 = _es.enter_context(nc.semaphore("ln_st1"))
        ss = _es.enter_context(nc.semaphore("ln_ss"))
        rp = _es.enter_context(nc.semaphore("ln_rp"))
        vd = _es.enter_context(nc.semaphore("ln_vd"))
        vp = _es.enter_context(nc.semaphore("ln_vp"))
        so0 = _es.enter_context(nc.semaphore("ln_so0"))
        so1 = _es.enter_context(nc.semaphore("ln_so1"))
        vcp = _es.enter_context(nc.semaphore("ln_vcp"))
        vsb = _es.enter_context(nc.semaphore("ln_vsb"))
        blk = _es.enter_context(nc.Block("ln"))
        @blk.sync
        def _(sync):
            for g in range(GN):
                b, t0 = g // 4, (g % 4) * TGW
                if g >= 2:
                    sync.wait_ge(vd, g - 1)
                half = (g % 2) * NCH * TGW
                for c0 in range(0, NCH, 4):
                    src = H["xt"][b][:, t0:t0 + TGW].rearrange("(c p) j -> p c j", p=128)[:, c0:c0 + 4, :]
                    dst = xg[:, half:half + NCH * TGW].rearrange("p (c j) -> p c j", c=NCH)[:, c0:c0 + 4, :]
                    sync.dma_start(dst, src).then_inc(xl0 if g % 2 == 0 else xl1, 16)

        @blk.vector
        def _(v):
            for g in range(GN):
                half = (g % 2) * NCH * TGW
                v.wait_ge(xl0 if g % 2 == 0 else xl1, 64 * (g // 2 + 1))
                if g >= 1:
                    v.wait_ge(vd, g)
                for c in range(NCH):
                    ins = v.tensor_copy(xbg[:, c * TGW:(c + 1) * TGW],
                                        xg[:, half + c * TGW: half + (c + 1) * TGW])
                    if c == NCH - 1:
                        ins.then_inc(vcp, 1)
                v.wait_ge(vcp, g + 1)
                for c in range(NCH):
                    ins = v.tensor_mul(x2g[:, c * TGW:(c + 1) * TGW],
                                       xbg[:, c * TGW:(c + 1) * TGW],
                                       xbg[:, c * TGW:(c + 1) * TGW])
                    if c == NCH - 1:
                        ins.then_inc(vc, 1)
                v.wait_ge(ts, g + 1)
                if g >= 1:
                    v.wait_ge(ss, g)
                v.tensor_copy(rows[:, 0:TGW], mps[0:1, :]).then_inc(vp, 1)
                v.tensor_copy(rows[:, TGW:2 * TGW], e2ps[0:1, :]).then_inc(vp, 1)
                v.wait_ge(vp, 4 * g + 2)
                v.tensor_mul(rows[:, 2 * TGW:3 * TGW], rows[:, 0:TGW], rows[:, 0:TGW]).then_inc(vp, 1)
                v.wait_ge(vp, 4 * g + 3)
                v.tensor_sub(rows[:, 3 * TGW:4 * TGW], rows[:, TGW:2 * TGW],
                             rows[:, 2 * TGW:3 * TGW]).then_inc(st1, 1)
                v.wait_ge(ss, g + 1)
                if g >= 1:
                    v.wait_ge(rp, g)
                v.reciprocal(rows[:, 5 * TGW:6 * TGW], rows[:, 4 * TGW:5 * TGW]).then_inc(vp, 1)
                v.wait_ge(vp, 4 * g + 4)
                v.tensor_copy(rows16[:, 0:TGW], rows[:, 0:TGW])
                v.tensor_copy(rows16[:, TGW:2 * TGW], rows[:, 5 * TGW:6 * TGW]).then_inc(st1, 1)
                v.wait_ge(rp, g + 1)
                if g >= 2:
                    v.wait_ge(so0 if g % 2 == 0 else so1, 64 * (g // 2))
                for c in range(NCH):
                    ins = v.tensor_sub(xg[:, half + c * TGW: half + (c + 1) * TGW],
                                       xg[:, half + c * TGW: half + (c + 1) * TGW], Mps[:, :])
                    if c == NCH - 1:
                        ins.then_inc(vsb, 1)
                v.wait_ge(vsb, g + 1)
                for c in range(NCH):
                    ins = v.tensor_mul(xng[:, half + c * TGW: half + (c + 1) * TGW],
                                       xg[:, half + c * TGW: half + (c + 1) * TGW], Rps[:, :])
                    if c == NCH - 1:
                        ins.then_inc(vd, 1)

        @blk.scalar
        def _(s):
            for g in range(GN):
                s.wait_ge(st1, 2 * g + 1)
                s.activation(rows[:, 4 * TGW:5 * TGW], rows[:, 3 * TGW:4 * TGW],
                             ACT.Sqrt, bias=S["eps"][0:1, 0:1], scale=1.0).then_inc(ss, 1)

        @blk.tensor
        def _(t):
            for g in range(GN):
                t.wait_ge(vc, g + 1)
                if g >= 1:
                    t.wait_ge(st1, 2 * (g - 1) + 1)
                for c in range(NCH):
                    t.matmul(mps[0:1, :], S["lnones"][:, 0:1],
                             xbg[:, c * TGW:(c + 1) * TGW], start=(c == 0), stop=(c == NCH - 1))
                for c in range(NCH):
                    ins = t.matmul(e2ps[0:1, :], S["lnones"][:, 0:1],
                                   x2g[:, c * TGW:(c + 1) * TGW], start=(c == 0), stop=(c == NCH - 1))
                    if c == NCH - 1:
                        ins.then_inc(ts, 1)
                t.wait_ge(st1, 2 * g + 2)
                if g >= 1:
                    t.wait_ge(vd, g)
                t.matmul(Mps[:, :], S["onesrow"][0:1, :], rows16[0:1, 0:TGW],
                         start=True, stop=True)
                t.matmul(Rps[:, :], S["onesrow"][0:1, :], rows16[0:1, TGW:2 * TGW],
                         start=True, stop=True).then_inc(rp, 1)

        @blk.gpsimd
        def _(gp):
            for g in range(GN):
                b, t0 = g // 4, (g % 4) * TGW
                gp.wait_ge(vd, g + 1)
                half = (g % 2) * NCH * TGW
                for c0 in range(0, NCH, 4):
                    dst = H["xn_h"][b][c0:c0 + 4, :, t0:t0 + TGW].rearrange("c p j -> p c j")
                    src = xng[:, half:half + NCH * TGW].rearrange("p (c j) -> p c j", c=NCH)[:, c0:c0 + 4, :]
                    gp.dma_start(dst, src).then_inc(so0 if g % 2 == 0 else so1, 16)
            gp.wait_ge(so0, 64 * GN // 2)
            gp.wait_ge(so1, 64 * GN // 2)


def phase_proj(nc, H, S, tag, w_name, in_name, out_name):
    GN = NTG
    with contextlib.ExitStack() as _es:
        wsb = _es.enter_context(nc.sbuf_tensor(tag + "_w", [128, NCH * D], BF16))
        ing = _es.enter_context(nc.sbuf_tensor(tag + "_in", [128, 2 * NCH * TGW], BF16))
        outg = _es.enter_context(nc.sbuf_tensor(tag + "_out", [128, 2 * NCH * TGW], BF16))
        p0 = _es.enter_context(nc.psum_tensor(tag + "_p0", [128, TGW], F32))
        p1 = _es.enter_context(nc.psum_tensor(tag + "_p1", [128, TGW], F32))
        wl = _es.enter_context(nc.semaphore(tag + "_wl"))
        il0 = _es.enter_context(nc.semaphore(tag + "_il0"))
        il1 = _es.enter_context(nc.semaphore(tag + "_il1"))
        mm = _es.enter_context(nc.semaphore(tag + "_mm"))
        eV = _es.enter_context(nc.semaphore(tag + "_eV"))
        eS = _es.enter_context(nc.semaphore(tag + "_eS"))
        so0 = _es.enter_context(nc.semaphore(tag + "_so0"))
        so1 = _es.enter_context(nc.semaphore(tag + "_so1"))
        blk = _es.enter_context(nc.Block(tag))
        psum = [p0, p1]

        @blk.gpsimd
        def _(gp):
            for c0 in range(0, NCH, 4):
                src = H[w_name][:, c0:c0 + 4, :]
                dst = wsb[:, :].rearrange("p (c o) -> p c o", c=NCH)[:, c0:c0 + 4, :]
                gp.dma_start(dst, src).then_inc(wl, 16)
            for g in range(GN):
                b, t0 = g // 4, (g % 4) * TGW
                gp.wait_ge(eV, 8 * (g + 1))
                gp.wait_ge(eS, 8 * (g + 1))
                half = (g % 2) * NCH * TGW
                for c0 in range(0, NCH, 4):
                    dst = H[out_name][b][c0:c0 + 4, :, t0:t0 + TGW].rearrange("c p j -> p c j")
                    src = outg[:, half:half + NCH * TGW].rearrange("p (c j) -> p c j", c=NCH)[:, c0:c0 + 4, :]
                    gp.dma_start(dst, src).then_inc(so0 if g % 2 == 0 else so1, 16)
            gp.wait_ge(so0, 64 * GN // 2)
            gp.wait_ge(so1, 64 * GN // 2)

        @blk.sync
        def _(sync):
            for g in range(GN):
                b, t0 = g // 4, (g % 4) * TGW
                if g >= 2:
                    sync.wait_ge(mm, 16 * (g - 1))
                half = (g % 2) * NCH * TGW
                for c0 in range(0, NCH, 4):
                    src = H[in_name][b][c0:c0 + 4, :, t0:t0 + TGW].rearrange("c p j -> p c j")
                    dst = ing[:, half:half + NCH * TGW].rearrange("p (c j) -> p c j", c=NCH)[:, c0:c0 + 4, :]
                    sync.dma_start(dst, src).then_inc(il0 if g % 2 == 0 else il1, 16)

        @blk.tensor
        def _(t):
            t.wait_ge(wl, 64)
            for g in range(GN):
                t.wait_ge(il0 if g % 2 == 0 else il1, 64 * (g // 2 + 1))
                half = (g % 2) * NCH * TGW
                for oc in range(NCH):
                    gi = g * NCH + oc
                    if gi >= 2:
                        t.wait_ge(eV if oc % 2 == 0 else eS, gi // 2)
                    ps = psum[oc % 2]
                    for c in range(NCH):
                        ins = t.matmul(ps[:, :],
                                       wsb[:, c * D + oc * 128: c * D + oc * 128 + 128],
                                       ing[:, half + c * TGW: half + (c + 1) * TGW],
                                       start=(c == 0), stop=(c == NCH - 1))
                        if c == NCH - 1:
                            ins.then_inc(mm, 1)

        @blk.vector
        def _(v):
            for g in range(GN):
                half = (g % 2) * NCH * TGW
                for oc in range(0, NCH, 2):
                    gi = g * NCH + oc
                    v.wait_ge(mm, gi + 1)
                    if g >= 2:
                        v.wait_ge(so0 if g % 2 == 0 else so1, 64 * (g // 2))
                    v.tensor_copy(outg[:, half + oc * TGW: half + (oc + 1) * TGW],
                                  psum[0][:, :]).then_inc(eV, 1)

        @blk.scalar
        def _(s):
            for g in range(GN):
                half = (g % 2) * NCH * TGW
                for oc in range(1, NCH, 2):
                    gi = g * NCH + oc
                    s.wait_ge(mm, gi + 1)
                    if g >= 2:
                        s.wait_ge(so0 if g % 2 == 0 else so1, 64 * (g // 2))
                    s.copy(outg[:, half + oc * TGW: half + (oc + 1) * TGW],
                           psum[1][:, :]).then_inc(eS, 1)


def phase_v(nc, H, S):
    GN = NTG
    with contextlib.ExitStack() as _es:
        wsb = _es.enter_context(nc.sbuf_tensor("v_w", [128, NCH * D], BF16))
        ing = _es.enter_context(nc.sbuf_tensor("v_in", [128, 2 * NCH * TGW], BF16))
        outg = _es.enter_context(nc.sbuf_tensor("v_out", [128, 2 * D], BF16))
        p0 = _es.enter_context(nc.psum_tensor("v_p0", [128, TGW], F32))
        p1 = _es.enter_context(nc.psum_tensor("v_p1", [128, TGW], F32))
        wl = _es.enter_context(nc.semaphore("v_wl"))
        il0 = _es.enter_context(nc.semaphore("v_il0"))
        il1 = _es.enter_context(nc.semaphore("v_il1"))
        mm = _es.enter_context(nc.semaphore("v_mm"))
        eV = _es.enter_context(nc.semaphore("v_eV"))
        eS = _es.enter_context(nc.semaphore("v_eS"))
        so0 = _es.enter_context(nc.semaphore("v_so0"))
        so1 = _es.enter_context(nc.semaphore("v_so1"))
        blk = _es.enter_context(nc.Block("vproj"))
        psum = [p0, p1]

        @blk.gpsimd
        def _(gp):
            for c0 in range(0, NCH, 4):
                src = H["wv"][:, c0:c0 + 4, :]
                dst = wsb[:, :].rearrange("p (c o) -> p c o", c=NCH)[:, c0:c0 + 4, :]
                gp.dma_start(dst, src).then_inc(wl, 16)
            for g in range(GN):
                b = g // 4
                for tt in range(4):
                    ti = g * 4 + tt
                    tc = (g % 4) * 4 + tt
                    gp.wait_ge(eV, 2 * (ti + 1))
                    gp.wait_ge(eS, 2 * (ti + 1))
                    src = outg[:, (ti % 2) * D:(ti % 2 + 1) * D]
                    gp.dma_start(H["v_h"][b][tc], src).then_inc(so0 if ti % 2 == 0 else so1, 16)
            gp.wait_ge(so0, 16 * 2 * GN)
            gp.wait_ge(so1, 16 * 2 * GN)

        @blk.sync
        def _(sync):
            for g in range(GN):
                b, t0 = g // 4, (g % 4) * TGW
                if g >= 2:
                    sync.wait_ge(mm, 16 * (g - 1))
                half = (g % 2) * NCH * TGW
                for c0 in range(0, NCH, 4):
                    src = H["xn_h"][b][c0:c0 + 4, :, t0:t0 + TGW].rearrange("c p j -> p c j")
                    dst = ing[:, half:half + NCH * TGW].rearrange("p (c j) -> p c j", c=NCH)[:, c0:c0 + 4, :]
                    sync.dma_start(dst, src).then_inc(il0 if g % 2 == 0 else il1, 16)

        @blk.tensor
        def _(t):
            t.wait_ge(wl, 64)
            for g in range(GN):
                t.wait_ge(il0 if g % 2 == 0 else il1, 64 * (g // 2 + 1))
                half = (g % 2) * NCH * TGW
                for tt in range(4):
                    for og in range(4):
                        ui = (g * 4 + tt) * 4 + og
                        if ui >= 2:
                            t.wait_ge(eV if ui % 2 == 0 else eS, ui // 2)
                        ps = psum[ui % 2]
                        for c in range(NCH):
                            ins = t.matmul(ps[:, :],
                                           ing[:, half + c * TGW + tt * 128: half + c * TGW + tt * 128 + 128],
                                           wsb[:, c * D + og * TGW: c * D + (og + 1) * TGW],
                                           start=(c == 0), stop=(c == NCH - 1))
                            if c == NCH - 1:
                                ins.then_inc(mm, 1)

        @blk.vector
        def _(v):
            for g in range(GN):
                for tt in range(4):
                    ti = g * 4 + tt
                    for og in range(4):
                        ui = ti * 4 + og
                        if ui % 2 != 0:
                            continue
                        v.wait_ge(mm, ui + 1)
                        if ti >= 2:
                            v.wait_ge(so0 if ti % 2 == 0 else so1, 16 * (ti // 2))
                        v.tensor_copy(outg[:, (ti % 2) * D + og * TGW:(ti % 2) * D + (og + 1) * TGW],
                                      psum[0][:, :]).then_inc(eV, 1)

        @blk.scalar
        def _(s):
            for g in range(GN):
                for tt in range(4):
                    ti = g * 4 + tt
                    for og in range(4):
                        ui = ti * 4 + og
                        if ui % 2 != 1:
                            continue
                        s.wait_ge(mm, ui + 1)
                        if ti >= 2:
                            s.wait_ge(so0 if ti % 2 == 0 else so1, 16 * (ti // 2))
                        s.copy(outg[:, (ti % 2) * D + og * TGW:(ti % 2) * D + (og + 1) * TGW],
                               psum[1][:, :]).then_inc(eS, 1)


def _attn_sched():
    out = []
    u = 0
    for qg in range(4):
        heads = []
        for hh in range(NH):
            nfull = 4 * qg
            lst = []
            ucol = 0
            if hh < NSTEEP:
                for p in range(0, nfull, 2):
                    lst.append((u, [p, p + 1], None, ucol)); u += 1; ucol += 1024
            else:
                for p in range(nfull):
                    lst.append((u, [p], None, ucol)); u += 1; ucol += 512
            for jj in range(4):
                lst.append((u, [nfull + jj], jj, ucol)); u += 1; ucol += 512
            heads.append(lst)
        out.append(heads)
    return out, u


def phase_attn(nc, H, S, b):
    tag = "at%d" % b
    SCHED, NU = _attn_sched()
    with contextlib.ExitStack() as _es:
        blbuf = _es.enter_context(nc.sbuf_tensor(tag + "_bl", [12, 2 * 16 * 128], BF16))
        brhs = _es.enter_context(nc.sbuf_tensor(tag + "_br", [12, 512], BF16))
        ebias = _es.enter_context(nc.sbuf_tensor(tag + "_eb", [128, (NH - NSTEEP) * 16], F32))
        tmask = _es.enter_context(nc.sbuf_tensor(tag + "_tm", [128, 128], BF16))
        ksb = _es.enter_context(nc.sbuf_tensor(tag + "_k", [128, NCH * T], BF16))
        vsb = _es.enter_context(nc.sbuf_tensor(tag + "_v", [128, NCH * T], BF16))
        qsb = _es.enter_context(nc.sbuf_tensor(tag + "_q", [128, NCH * TGW], BF16))
        psb = _es.enter_context(nc.sbuf_tensor(tag + "_p", [128, 16 * TGW], BF16))
        obuf = _es.enter_context(nc.sbuf_tensor(tag + "_ob", [128, 4 * TGW], F32))
        sigrows = _es.enter_context(nc.sbuf_tensor(tag + "_sg", [4, TGW], F32))
        sig0 = _es.enter_context(nc.sbuf_tensor(tag + "_s0r", [1, 2 * 4 * TGW], F32))
        rr = _es.enter_context(nc.sbuf_tensor(tag + "_rr", [4, TGW], F32))
        rrb = _es.enter_context(nc.sbuf_tensor(tag + "_rb", [1, 4 * TGW], BF16))
        oq = _es.enter_context(nc.sbuf_tensor(tag + "_oq", [128, 2 * TGW], BF16))
        s0 = _es.enter_context(nc.psum_tensor(tag + "_s0", [128, 1024], F32))
        s1 = _es.enter_context(nc.psum_tensor(tag + "_s1", [128, 1024], F32))
        ops = _es.enter_context(nc.psum_tensor(tag + "_o", [128, TGW], F32))
        sigps = _es.enter_context(nc.psum_tensor(tag + "_sig", [1, TGW], F32))
        rps = _es.enter_context(nc.psum_tensor(tag + "_r", [128, TGW], F32))
        kvl = _es.enter_context(nc.semaphore(tag + "_kv"))
        bl0 = _es.enter_context(nc.semaphore(tag + "_bls0"))
        bl1 = _es.enter_context(nc.semaphore(tag + "_bls1"))
        ql0 = _es.enter_context(nc.semaphore(tag + "_ql0"))
        ql1 = _es.enter_context(nc.semaphore(tag + "_ql1"))
        sm = _es.enter_context(nc.semaphore(tag + "_sm"))
        ex = _es.enter_context(nc.semaphore(tag + "_ex"))
        mk = _es.enter_context(nc.semaphore(tag + "_mk"))
        pv = _es.enter_context(nc.semaphore(tag + "_pv"))
        oc = _es.enter_context(nc.semaphore(tag + "_oc"))
        nr = _es.enter_context(nc.semaphore(tag + "_nr"))
        nrd0 = _es.enter_context(nc.semaphore(tag + "_nrd0"))
        nrd1 = _es.enter_context(nc.semaphore(tag + "_nrd1"))
        sgd = _es.enter_context(nc.semaphore(tag + "_sgd"))
        rp = _es.enter_context(nc.semaphore(tag + "_rp"))
        nm = _es.enter_context(nc.semaphore(tag + "_nm"))
        so0 = _es.enter_context(nc.semaphore(tag + "_so0"))
        so1 = _es.enter_context(nc.semaphore(tag + "_so1"))
        blk = _es.enter_context(nc.Block(tag))
        spsum = [s0, s1]

        @blk.sync
        def _(sync):
            for k, dst in (("biasrhs", brhs),
                           ("expbias", ebias), ("trimask", tmask)):
                sync.dma_start(dst[:, :], H[k][:, :]).then_inc(kvl, 16)
            for c0 in range(0, NCH, 4):
                src = H["k_h"][b][c0:c0 + 4].rearrange("c p j -> p c j")
                dst = ksb[:, :].rearrange("p (c j) -> p c j", c=NCH)[:, c0:c0 + 4, :]
                sync.dma_start(dst, src).then_inc(kvl, 16)
            for c0 in range(0, NCH, 4):
                src = H["v_h"][b][c0:c0 + 4].rearrange("c p j -> p c j")
                dst = vsb[:, :].rearrange("p (c j) -> p c j", c=NCH)[:, c0:c0 + 4, :]
                sync.dma_start(dst, src).then_inc(kvl, 16)
            for qg in range(4):
                if qg >= 1:
                    sync.wait_ge(sm, SCHED[qg - 1][NH - 1][-1][0] + 1)
                t0 = qg * TGW
                for c0 in range(0, NCH, 4):
                    src = H["q_h"][b][c0:c0 + 4, :, t0:t0 + TGW].rearrange("c p j -> p c j")
                    dst = qsb[:, :].rearrange("p (c j) -> p c j", c=NCH)[:, c0:c0 + 4, :]
                    sync.dma_start(dst, src).then_inc(ql0 if qg % 2 == 0 else ql1, 16)
                for hh in range(NSTEEP):
                    li = qg * NSTEEP + hh
                    if li >= 2:
                        pqg, phh = divmod(li - 2, NSTEEP)
                        sync.wait_ge(sm, SCHED[pqg][phh][-1][0] + 1)
                    sync.dma_start(blbuf[:, (li % 2) * 2048:(li % 2 + 1) * 2048],
                                   H["biaslhs"][:, hh * 2048:(hh + 1) * 2048]).then_inc(bl0 if li % 2 == 0 else bl1, 16)

        @blk.tensor
        def _(t):
            def _emit_repl(t, qidx):
                t.wait_ge(nrd0 if qidx % 2 == 0 else nrd1, 64 * (qidx // 2 + 1))
                for hl in range(4):
                    gh = qidx * 4 + hl
                    if gh >= 1:
                        t.wait_ge(nm, gh)
                    t.matmul(rps[:, :], S["onesrow"][0:1, :],
                             rrb[0:1, hl * TGW:(hl + 1) * TGW],
                             start=True, stop=True).then_inc(rp, 1)
            t.wait_ge(kvl, 176)
            for qg in range(4):
                t.wait_ge(ql0 if qg % 2 == 0 else ql1, 64 * (qg // 2 + 1))
                for hh in range(NH):
                    units = SCHED[qg][hh]
                    head_idx = qg * NH + hh
                    nkc = 4 * qg + 4
                    cur_q = qg * 4 + hh // 4
                    if hh % 4 == 0 and cur_q >= 1:
                        _emit_repl(t, cur_q - 1)
                    if hh < NSTEEP:
                        li = qg * NSTEEP + hh
                        t.wait_ge(bl0 if li % 2 == 0 else bl1, 16 * (li // 2 + 1))
                    for (u, kcs, dj, ucol) in units:
                        slot = u % 2
                        if u >= 2:
                            t.wait_ge(ex, u - 1)
                        ins = None
                        for idx, kc in enumerate(kcs):
                            j0 = 0 if dj is None else dj * 128
                            width = TGW - j0
                            col0 = idx * 512 + j0
                            ins = t.matmul(
                                spsum[slot][:, col0:col0 + width],
                                ksb[:, hh * T + kc * 128: hh * T + kc * 128 + 128],
                                qsb[:, hh * TGW + j0: hh * TGW + TGW],
                                start=True, stop=(hh >= NSTEEP))
                            if hh < NSTEEP:
                                o = kc - 4 * qg + 12
                                ins = t.matmul(
                                    spsum[slot][:, col0:col0 + width],
                                    blbuf[:, ((qg * NSTEEP + hh) % 2) * 2048 + o * 128:
                                          ((qg * NSTEEP + hh) % 2) * 2048 + o * 128 + 128],
                                    brhs[:, j0:TGW],
                                    start=False, stop=True)
                        ins.then_inc(sm, 1)
                    if head_idx >= 1:
                        t.wait_ge(oc, 2 * head_idx)
                    first = True
                    ins = None
                    for (u, kcs, dj, ucol) in units:
                        if dj is None:
                            t.wait_ge(ex, u + 1)
                        else:
                            t.wait_ge(mk, head_idx * 4 + dj + 1)
                        for idx, kc in enumerate(kcs):
                            j0 = 0 if dj is None else dj * 128
                            width = TGW - j0
                            pcol = ucol + idx * 512 + j0
                            last = (kc == nkc - 1)
                            t.matmul(ops[:, j0:TGW],
                                     vsb[:, kc * T + hh * 128: kc * T + hh * 128 + 128],
                                     psb[:, pcol:pcol + width],
                                     start=first, stop=last)
                            ins = t.matmul(sigps[0:1, j0:TGW],
                                           S["onescol"][:, 0:1],
                                           psb[:, pcol:pcol + width],
                                           start=first, stop=last)
                            first = False
                    ins.then_inc(pv, 1)
                    if head_idx == NH * 4 - 1:
                        _emit_repl(t, 15)

        @blk.scalar
        def _(s):
            for qg in range(4):
                for hh in range(NH):
                    for (u, kcs, dj, ucol) in SCHED[qg][hh]:
                        s.wait_ge(sm, u + 1)
                        slot = u % 2
                        j0 = 0 if dj is None else dj * 128
                        if len(kcs) == 2:
                            s.activation(psb[:, ucol:ucol + 1024],
                                         spsum[slot][:, 0:1024], ACT.Exp,
                                         bias=0.0, scale=INV_SQRT_HD).then_inc(ex, 1)
                        elif hh < NSTEEP:
                            s.activation(psb[:, ucol + j0:ucol + TGW],
                                         spsum[slot][:, j0:TGW], ACT.Exp,
                                         bias=0.0, scale=INV_SQRT_HD).then_inc(ex, 1)
                        else:
                            o = kcs[0] - 4 * qg + 12
                            bidx = (hh - NSTEEP) * 16 + o
                            s.activation(psb[:, ucol + j0:ucol + TGW],
                                         spsum[slot][:, j0:TGW], ACT.Exp,
                                         bias=ebias[:, bidx:bidx + 1],
                                         scale=INV_SQRT_HD).then_inc(ex, 1)

        @blk.vector
        def _(v):
            for qg in range(4):
                for hh in range(NH):
                    units = SCHED[qg][hh]
                    head_idx = qg * NH + hh
                    for (u, kcs, dj, ucol) in units:
                        if dj is not None:
                            v.wait_ge(ex, u + 1)
                            base = ucol + dj * 128
                            v.tensor_mul(psb[:, base:base + 128],
                                         psb[:, base:base + 128],
                                         tmask[:, :]).then_inc(mk, 1)
                    v.wait_ge(pv, head_idx + 1)
                    hl4 = hh % 4
                    qidx = qg * 4 + hh // 4
                    if hl4 == 0 and qidx >= 2:
                        v.wait_ge(sgd, 64 * (qidx - 1))
                    s0c = (qidx % 2) * 4 * TGW
                    v.tensor_copy(obuf[:, hl4 * TGW:(hl4 + 1) * TGW], ops[:, :]).then_inc(oc, 1)
                    v.tensor_copy(sig0[0:1, s0c + hl4 * TGW: s0c + (hl4 + 1) * TGW],
                                  sigps[0:1, :]).then_inc(oc, 1)
                    if hh % 4 == 3:
                        if qidx >= 2:
                            v.wait_ge(nrd0 if qidx % 2 == 0 else nrd1, 64 * (qidx // 2))
                        v.wait_ge(sgd, 64 * (qidx + 1))
                        v.reciprocal(rr[:, :], sigrows[:, :]).then_inc(nr, 1)
                        for hl in range(4):
                            gh = qidx * 4 + hl
                            v.wait_ge(rp, gh + 1)
                            if gh >= 2:
                                v.wait_ge(so0 if gh % 2 == 0 else so1, 16 * (gh // 2))
                            v.tensor_mul(oq[:, (gh % 2) * TGW:(gh % 2 + 1) * TGW],
                                         obuf[:, hl * TGW:(hl + 1) * TGW],
                                         rps[:, :]).then_inc(nm, 1)

        @blk.gpsimd
        def _(gp):
            for qg in range(4):
                t0 = qg * TGW
                for qb in range(4):
                    qidx = qg * 4 + qb
                    gp.wait_ge(oc, 2 * (4 * qidx + 4))
                    if qidx >= 1:
                        gp.wait_ge(nr, qidx)
                    s0c = (qidx % 2) * 4 * TGW
                    for hl in range(4):
                        gp.dma_start(sigrows[hl:hl + 1, :],
                                     sig0[0:1, s0c + hl * TGW: s0c + (hl + 1) * TGW]).then_inc(sgd, 16)
                    gp.wait_ge(nr, qidx + 1)
                    if qidx >= 1:
                        gp.wait_ge(rp, 4 * qidx)
                    for hl in range(4):
                        gp.dma_start(rrb[0:1, hl * TGW:(hl + 1) * TGW],
                                     rr[hl:hl + 1, :]).then_inc(nrd0 if qidx % 2 == 0 else nrd1, 16)
                    for hl in range(4):
                        gh = qidx * 4 + hl
                        hh = qb * 4 + hl
                        gp.wait_ge(nm, gh + 1)
                        gp.dma_start(H["o_h"][b][hh][:, t0:t0 + TGW],
                                     oq[:, (gh % 2) * TGW:(gh % 2 + 1) * TGW]).then_inc(so0 if gh % 2 == 0 else so1, 16)
            gp.wait_ge(so0, 16 * 32)
            gp.wait_ge(so1, 16 * 32)


def phase_f1(nc, H, S):
    NT = NB * T // TGW
    with contextlib.ExitStack() as _es:
        xnf = _es.enter_context(nc.sbuf_tensor("f1_xn", [128, NCH * NB * T], BF16))
        wst = _es.enter_context(nc.sbuf_tensor("f1_w", [128, 2 * D], BF16))
        ysb = _es.enter_context(nc.sbuf_tensor("f1_y", [128, 2 * TGW], F32))
        q0sb = _es.enter_context(nc.sbuf_tensor("f1_q0", [128, 2 * TGW], I16))
        dsb = _es.enter_context(nc.sbuf_tensor("f1_d", [128, 2 * TGW], F32))
        tA = _es.enter_context(nc.sbuf_tensor("f1_ta", [128, 2 * TGW], F32))
        tB = _es.enter_context(nc.sbuf_tensor("f1_tb", [128, 2 * TGW], F32))
        absf = _es.enter_context(nc.sbuf_tensor("f1_a", [128, 2 * NB * T], BF16))
        p0 = _es.enter_context(nc.psum_tensor("f1_p0", [128, TGW], F32))
        p1 = _es.enter_context(nc.psum_tensor("f1_p1", [128, TGW], F32))
        xl = _es.enter_context(nc.semaphore("f1_xl"))
        wl0 = _es.enter_context(nc.semaphore("f1_wl0"))
        wl1 = _es.enter_context(nc.semaphore("f1_wl1"))
        mm = _es.enter_context(nc.semaphore("f1_mm"))
        ys = _es.enter_context(nc.semaphore("f1_ys"))
        vq = _es.enter_context(nc.semaphore("f1_vq"))
        vp = _es.enter_context(nc.semaphore("f1_vp"))
        ge = _es.enter_context(nc.semaphore("f1_ge"))
        so0 = _es.enter_context(nc.semaphore("f1_so0"))
        so1 = _es.enter_context(nc.semaphore("f1_so1"))
        blk = _es.enter_context(nc.Block("f1"))
        psum = [p0, p1]

        @blk.sync
        def _(sync):
            for bb in range(NB):
                for c0 in range(0, NCH, 4):
                    src = H["xn_h"][bb][c0:c0 + 4].rearrange("c p j -> p c j")
                    dst = xnf[:, :].rearrange("p (c j) -> p c j", c=NCH)[
                        :, c0:c0 + 4, bb * T:(bb + 1) * T]
                    sync.dma_start(dst, src).then_inc(xl, 16)

        @blk.gpsimd
        def _(gp):
            for fc in range(2):
                gp.dma_start(wst[:, (fc % 2) * D:(fc % 2 + 1) * D],
                             H["w1t"][fc]).then_inc(wl0 if fc % 2 == 0 else wl1, 16)
            for fc in range(NFC):
                if fc + 2 < NFC:
                    gp.wait_ge(mm, NT * (fc + 1))
                    gp.dma_start(wst[:, (fc % 2) * D:(fc % 2 + 1) * D],
                                 H["w1t"][fc + 2]).then_inc(wl0 if fc % 2 == 0 else wl1, 16)
                gp.wait_ge(ge, NT * (fc + 1))
                for g in range(NT):
                    gp.dma_start(H["a_h"][g, :, fc, :],
                                 absf[:, (fc % 2) * NB * T + g * TGW:
                                      (fc % 2) * NB * T + (g + 1) * TGW]).then_inc(so0 if fc % 2 == 0 else so1, 16)
            gp.wait_ge(so0, 16 * NT * NFC // 2)
            gp.wait_ge(so1, 16 * NT * NFC // 2)

        @blk.tensor
        def _(t):
            t.wait_ge(xl, 16 * 8)
            for fc in range(NFC):
                t.wait_ge(wl0 if fc % 2 == 0 else wl1, 16 * (fc // 2 + 1))
                for tg in range(NT):
                    ui = fc * NT + tg
                    if ui >= 2:
                        t.wait_ge(ys, ui - 1)
                    ps = psum[ui % 2]
                    for c in range(NCH):
                        ins = t.matmul(ps[:, :],
                                       wst[:, (fc % 2) * D + c * 128:(fc % 2) * D + (c + 1) * 128],
                                       xnf[:, c * NB * T + tg * TGW: c * NB * T + (tg + 1) * TGW],
                                       start=(c == 0), stop=(c == NCH - 1))
                        if c == NCH - 1:
                            ins.then_inc(mm, 1)

        @blk.scalar
        def _(s):
            NTOT = NFC * NT
            for ui in range(NTOT + 1):
                if ui < NTOT:
                    hp = ui % 2
                    fc = ui // NT
                    s.wait_ge(mm, ui + 1)
                    if ui >= 2:
                        s.wait_ge(vq, ui - 1)
                    s.activation(q0sb[:, hp * TGW:(hp + 1) * TGW], psum[hp][:, :],
                                 ACT.Copy, bias=0.0, scale=10.0)
                    s.activation(ysb[:, hp * TGW:(hp + 1) * TGW], psum[hp][:, :],
                                 ACT.Copy, bias=0.0, scale=10.0).then_inc(ys, 1)
                if ui >= 1:
                    uj = ui - 1
                    hp = uj % 2
                    fc = uj // NT
                    tg = uj % NT
                    s.wait_ge(vq, uj + 1)
                    if fc >= 2 and tg == 0:
                        s.wait_ge(so0 if fc % 2 == 0 else so1, 16 * NT * (fc // 2))
                    s.activation(absf[:, (fc % 2) * NB * T + tg * TGW:(fc % 2) * NB * T + (tg + 1) * TGW],
                                 tA[:, hp * TGW:(hp + 1) * TGW],
                                 ACT.Gelu_apprx_tanh, bias=0.0, scale=-0.1).then_inc(ge, 1)

        @blk.vector
        def _(v):
            NTOT = NFC * NT

            def emit(u, k):
                hp = u % 2
                ysl = ysb[:, hp * TGW:(hp + 1) * TGW]
                q0 = q0sb[:, hp * TGW:(hp + 1) * TGW]
                dd = dsb[:, hp * TGW:(hp + 1) * TGW]
                a_ = tA[:, hp * TGW:(hp + 1) * TGW]
                b_ = tB[:, hp * TGW:(hp + 1) * TGW]
                if k == 0:
                    return v.tensor_sub(dd, ysl, q0)
                if k == 1:
                    return v.scalar_tensor_tensor(a_, dd, 0.5, q0, ALU.is_ge, ALU.add)
                if k == 2:
                    return v.scalar_tensor_tensor(b_, dd, -0.5, a_, ALU.is_le, ALU.subtract)
                return v.tensor_scalar(a_, b_, 32768.0, -32767.0, ALU.min, ALU.max)

            for p_ in range(NTOT // 2):
                u0, u1 = 2 * p_, 2 * p_ + 1
                base = 6 * p_
                v.wait_ge(ys, u0 + 1)
                emit(u0, 0).then_inc(vp, 1)
                v.wait_ge(ys, u1 + 1)
                emit(u1, 0).then_inc(vp, 1)
                for k in range(1, 3):
                    if k == 1:
                        if u0 >= 2:
                            v.wait_ge(ge, u0 - 1)
                    v.wait_ge(vp, base + 2 * (k - 1) + 1)
                    emit(u0, k).then_inc(vp, 1)
                    if k == 1 and u1 >= 2:
                        v.wait_ge(ge, u1 - 1)
                    v.wait_ge(vp, base + 2 * (k - 1) + 2)
                    emit(u1, k).then_inc(vp, 1)
                v.wait_ge(vp, base + 5)
                emit(u0, 3).then_inc(vq, 1)
                v.wait_ge(vp, base + 6)
                emit(u1, 3).then_inc(vq, 1)


def phase_f2(nc, H, S):
    NT = NB * T // TGW
    with contextlib.ExitStack() as _es:
        wsb = _es.enter_context(nc.sbuf_tensor("f2_w", [128, 2 * NFC * TGW], BF16))
        asb = _es.enter_context(nc.sbuf_tensor("f2_a", [128, 16 * TGW], BF16))
        ssb = _es.enter_context(nc.sbuf_tensor("f2_s", [128, 2 * 4 * TGW], BF16))
        xsb = _es.enter_context(nc.sbuf_tensor("f2_x", [128, 2 * 4 * TGW], F32))
        gsb = _es.enter_context(nc.sbuf_tensor("f2_g", [128, 2 * TGW], BF16))
        sgsb = _es.enter_context(nc.sbuf_tensor("f2_sg", [128, 2 * TGW], F32))
        osb = _es.enter_context(nc.sbuf_tensor("f2_o", [128, 8 * TGW], F32))
        q0 = _es.enter_context(nc.psum_tensor("f2_p0", [128, TGW], F32))
        q1 = _es.enter_context(nc.psum_tensor("f2_p1", [128, TGW], F32))
        q2 = _es.enter_context(nc.psum_tensor("f2_p2", [128, TGW], F32))
        q3 = _es.enter_context(nc.psum_tensor("f2_p3", [128, TGW], F32))
        q4 = _es.enter_context(nc.psum_tensor("f2_p4", [128, TGW], F32))
        q5 = _es.enter_context(nc.psum_tensor("f2_p5", [128, TGW], F32))
        q6 = _es.enter_context(nc.psum_tensor("f2_p6", [128, TGW], F32))
        q7 = _es.enter_context(nc.psum_tensor("f2_p7", [128, TGW], F32))
        wl0 = _es.enter_context(nc.semaphore("f2_wl0"))
        wl1 = _es.enter_context(nc.semaphore("f2_wl1"))
        al0 = _es.enter_context(nc.semaphore("f2_al0"))
        al1 = _es.enter_context(nc.semaphore("f2_al1"))
        al2 = _es.enter_context(nc.semaphore("f2_al2"))
        al3 = _es.enter_context(nc.semaphore("f2_al3"))
        aq = _es.enter_context(nc.semaphore("f2_aq"))
        sx0 = _es.enter_context(nc.semaphore("f2_sx0"))
        sx1 = _es.enter_context(nc.semaphore("f2_sx1"))
        si = _es.enter_context(nc.semaphore("f2_si"))
        vp2 = _es.enter_context(nc.semaphore("f2_vp"))
        cb = _es.enter_context(nc.semaphore("f2_cb"))
        so0 = _es.enter_context(nc.semaphore("f2_so0"))
        so1 = _es.enter_context(nc.semaphore("f2_so1"))
        blk = _es.enter_context(nc.Block("f2"))
        psums = [q0, q1, q2, q3, q4, q5, q6, q7]
        HW = NFC * TGW // 2

        @blk.gpsimd
        def _(gp):
            WS = NFC * TGW
            for h in range(2):
                gp.dma_start(wsb[:, h * HW:h * HW + HW],
                             H["w2t"][0][:, h * HW:(h + 1) * HW]).then_inc(wl0, 16)
            for st in range(4):
                for g in range(NT):
                    if g == 0 and st + 1 < 4:
                        if st >= 1:
                            gp.wait_ge(aq, 256 * NT * st)
                        wofs = ((st + 1) % 2) * WS
                        for h in range(2):
                            gp.dma_start(wsb[:, wofs + h * HW:wofs + h * HW + HW],
                                         H["w2t"][st + 1][:, h * HW:(h + 1) * HW]).then_inc(wl0 if (st + 1) % 2 == 0 else wl1, 16)
                    bb, t0 = g // 4, (g % 4) * TGW
                    gi = st * NT + g
                    for ocl in range(4):
                        oi = gi * 4 + ocl
                        gp.wait_ge(cb, oi + 1)
                        d0 = st * 512 + ocl * 128
                        gp.dma_start(H["outt"][bb][d0:d0 + 128, t0:t0 + TGW],
                                     osb[:, ((gi % 2) * 4 + ocl) * TGW:((gi % 2) * 4 + ocl + 1) * TGW]).then_inc(so0 if gi % 2 == 0 else so1, 16)
            gp.wait_ge(so0, 16 * 2 * NT * 4)
            gp.wait_ge(so1, 16 * 2 * NT * 4)

        @blk.sync
        def _(sync):
            for st in range(4):
                for g in range(NT):
                    bb, t0 = g // 4, (g % 4) * TGW
                    gi = st * NT + g
                    if gi >= 2:
                        sync.wait_ge(cb, 4 * (gi - 1))
                    sh = (gi % 2) * 4 * TGW
                    sxp = sx0 if gi % 2 == 0 else sx1
                    src = H["s_h"][bb][4 * st:4 * st + 4, :, t0:t0 + TGW].rearrange("c p j -> p c j")
                    dst = ssb[:, sh:sh + 4 * TGW].rearrange("p (c j) -> p c j", c=4)
                    sync.dma_start(dst, src).then_inc(sxp, 16)
                    src = H["xt"][bb][st * 512:(st + 1) * 512, t0:t0 + TGW].rearrange("(c p) j -> p c j", p=128)
                    dst = xsb[:, sh:sh + 4 * TGW].rearrange("p (c j) -> p c j", c=4)
                    sync.dma_start(dst, src).then_inc(sxp, 16)
                    for f0 in range(0, NFC, 4):
                        qi = gi * 16 + f0 // 4
                        if qi >= 4:
                            sync.wait_ge(aq, 16 * (qi - 3))
                        src = H["a_h"][g, :, f0:f0 + 4, :]
                        dst = asb[:, (qi % 4) * 4 * TGW:(qi % 4 + 1) * 4 * TGW].rearrange(
                            "p (f j) -> p f j", f=4)
                        sync.dma_start(dst, src).then_inc([al0, al1, al2, al3][qi % 4], 16)

        @blk.tensor
        def _(t):
            for st in range(4):
                t.wait_ge(wl0 if st % 2 == 0 else wl1, 32 * (st // 2 + 1))
                for g in range(NT):
                    gi = st * NT + g
                    pb = (gi % 2) * 4
                    if gi >= 2:
                        t.wait_ge(si, 4 * (gi - 1))
                    for f0 in range(0, NFC, 4):
                        qi = gi * 16 + f0 // 4
                        t.wait_ge([al0, al1, al2, al3][qi % 4], 16 * (qi // 4 + 1))
                        for ff in range(4):
                            fc = f0 + ff
                            wofs = (st % 2) * (NFC * TGW)
                            for ocl in range(4):
                                ins = t.matmul(psums[pb + ocl][:, :],
                                               wsb[:, wofs + fc * TGW + ocl * 128: wofs + fc * TGW + ocl * 128 + 128],
                                               asb[:, (qi % 4) * 4 * TGW + ff * TGW:(qi % 4) * 4 * TGW + (ff + 1) * TGW],
                                               start=(fc == 0), stop=(fc == NFC - 1))
                                if ff == 3 and ocl == 3:
                                    ins.then_inc(aq, 16)

        @blk.scalar
        def _(s):
            for st in range(4):
                for g in range(NT):
                    gi = st * NT + g
                    pb = (gi % 2) * 4
                    s.wait_ge(aq, 256 * (gi + 1))
                    for ocl in range(4):
                        oi = gi * 4 + ocl
                        if oi >= 2:
                            s.wait_ge(cb, oi - 1)
                        s.activation(gsb[:, (oi % 2) * TGW:(oi % 2 + 1) * TGW],
                                     psums[pb + ocl][:, :], ACT.Silu,
                                     bias=0.0, scale=1.0).then_inc(si, 1)

        @blk.vector
        def _(v):
            for st in range(4):
                for g in range(NT):
                    gi = st * NT + g
                    sh = (gi % 2) * 4 * TGW
                    v.wait_ge(sx0 if gi % 2 == 0 else sx1, 32 * (gi // 2 + 1))
                    for ocl in range(4):
                        oi = gi * 4 + ocl
                        v.wait_ge(si, oi + 1)
                        if gi >= 2:
                            v.wait_ge(so0 if gi % 2 == 0 else so1, 16 * 4 * (gi // 2))
                        v.tensor_mul(sgsb[:, (oi % 2) * TGW:(oi % 2 + 1) * TGW],
                                     gsb[:, (oi % 2) * TGW:(oi % 2 + 1) * TGW],
                                     ssb[:, sh + ocl * TGW: sh + (ocl + 1) * TGW]).then_inc(vp2, 1)
                        v.wait_ge(vp2, oi + 1)
                        v.tensor_add(osb[:, ((gi % 2) * 4 + ocl) * TGW:((gi % 2) * 4 + ocl + 1) * TGW],
                                     sgsb[:, (oi % 2) * TGW:(oi % 2 + 1) * TGW],
                                     xsb[:, sh + ocl * TGW: sh + (ocl + 1) * TGW]).then_inc(cb, 1)


def _phase_reset(nc):
    nc.reset(previous_sbuf_base=nc.sbuf_base, previous_sbuf_top=nc.sbuf_top,
             previous_psum_base=nc.psum_base, previous_psum_top=nc.psum_top,
             previous_local_dram_base=nc.local_dram_base,
             previous_shared_dram_base=nc.shared_dram_base)


def build_nc(phases=("consts", "ln", "q", "k", "v", "attn", "s", "f1", "f2")):
    import contextlib
    nc = bass.Bass("TRN2", target_bir_lowering=False)
    H = _dram_io(nc)
    with contextlib.ExitStack() as stack:
        S = {}
        S["onescol"] = stack.enter_context(nc.sbuf_tensor("c_onescol", [128, 1], BF16))
        S["lnones"] = stack.enter_context(nc.sbuf_tensor("c_lnones", [128, 1], BF16))
        S["onesrow"] = stack.enter_context(nc.sbuf_tensor("c_onesrow", [1, 128], BF16))
        S["eps"] = stack.enter_context(nc.sbuf_tensor("c_eps", [1, 1], F32))
        first = True
        def sep():
            nonlocal first
            if not first:
                _phase_reset(nc)
            first = False
        if "consts" in phases:
            sep(); phase_consts(nc, H, S)
        if "ln" in phases:
            sep(); phase_ln(nc, H, S)
        if "q" in phases:
            sep(); phase_proj(nc, H, S, "pq", "wq", "xn_h", "q_h")
        if "k" in phases:
            sep(); phase_proj(nc, H, S, "pk", "wk", "xn_h", "k_h")
        if "v" in phases:
            sep(); phase_v(nc, H, S)
        if "attn" in phases:
            sep(); phase_attn(nc, H, S, 0)
            sep(); phase_attn(nc, H, S, 1)
        if "s" in phases:
            sep(); phase_proj(nc, H, S, "po", "wo", "o_h", "s_h")
        if "f1" in phases:
            sep(); phase_f1(nc, H, S)
        if "f2" in phases:
            sep(); phase_f2(nc, H, S)
    return nc


_CONSTS = None
_NC_CACHE = None


def _host_inputs(L0_attn_x, Wq, Wk, Wv, Wo, W1, W2):
    global _CONSTS
    if _CONSTS is None:
        _CONSTS = build_consts()
    C = _CONSTS
    base = {}
    for nm, W in (("wq", Wq), ("wk", Wk), ("wv", Wv), ("wo", Wo)):
        base[nm] = np.ascontiguousarray(
            np.asarray(W, dtype=np.float32).reshape(D, NCH, 128).transpose(2, 1, 0))
    base["w1t"] = np.ascontiguousarray(
        np.asarray(W1, dtype=np.float32).reshape(NFC, 128, NCH, 128).transpose(0, 3, 2, 1)).reshape(NFC, 128, D)
    base["w2t"] = np.ascontiguousarray(
        np.asarray(W2, dtype=np.float32).reshape(4, 512, NFC, 128).transpose(0, 3, 2, 1)).reshape(4, 128, NFC * 512)
    base.update({k: C[k] for k in C})
    x = np.asarray(L0_attn_x, dtype=np.float32)
    in_maps = []
    for c in range(8):
        m = dict(base)
        m["xt"] = np.ascontiguousarray(x[2 * c:2 * c + 2].transpose(0, 2, 1))
        in_maps.append(m)
    return in_maps


def kernel(L0_attn_x, L0_attn_Wq, L0_attn_Wk, L0_attn_Wv, L0_attn_Wo,
           L0_ffn_W1, L0_ffn_W2):
    global _NC_CACHE
    in_maps = _host_inputs(L0_attn_x, L0_attn_Wq, L0_attn_Wk, L0_attn_Wv,
                           L0_attn_Wo, L0_ffn_W1, L0_ffn_W2)
    if _NC_CACHE is None:
        _NC_CACHE = build_nc()
    res = run_bass_kernel_spmd(_NC_CACHE, in_maps, core_ids=list(range(8)))
    out = np.empty((16, T, D), dtype=np.float32)
    for c in range(8):
        r = np.asarray(res.results[c]["outt"])
        out[2 * c:2 * c + 2] = r.transpose(0, 2, 1)
    return out


# version 07 (final)
# speedup vs baseline: 1.0337x; 1.0337x over previous
import math
import contextlib
import numpy as np
import ml_dtypes

import concourse.bass as bass
import concourse.mybir as mybir
from concourse.bass_utils import run_bass_kernel_spmd

F32 = mybir.dt.float32
BF16 = mybir.dt.bfloat16
I16 = mybir.dt.int16
ALU = mybir.AluOpType
ACT = mybir.ActivationFunctionType

D = 2048
T = 2048
NB = 2
NCH = 16
NH = 16
FF = 8192
NFC = 64
TGW = 512
NTG = NB * (T // TGW)
NSTEEP = 7
QREF = 292
INV_SQRT_HD = float(1.0 / math.sqrt(128.0))

_BF16 = ml_dtypes.bfloat16


def _split4(v):
    parts = []
    r = np.array(v, dtype=np.float64)
    for _ in range(4):
        p = r.astype(_BF16)
        parts.append(p)
        r = r - p.astype(np.float64)
    return parts


def _slopes64():
    h = np.arange(1, NH + 1, dtype=np.float64)
    return 0.5 ** (h * 8.0 / NH)


def build_consts():
    sl = _slopes64()
    A = math.sqrt(128.0) * sl
    C = np.maximum(0.0, sl * 127.0 - 75.0)

    onescol = np.ones((128, 1), dtype=_BF16)
    lnones = np.full((128, 1), 1.0 / 2048.0, dtype=_BF16)
    onesrow = np.ones((1, 128), dtype=_BF16)
    tri = (np.arange(128)[:, None] <= np.arange(128)[None, :]).astype(_BF16)

    j = np.arange(512, dtype=np.float64)
    brhs = np.zeros((12, 512), dtype=_BF16)
    brhs[0:4] = _BF16(1.0)
    for r in range(4, 8):
        brhs[r] = (j - 256.0).astype(_BF16)
    brhs[8:12] = _BF16(1.0)

    blhs = np.zeros((12, NSTEEP * 16 * 128), dtype=_BF16)
    i = np.arange(128, dtype=np.float64)
    for hh in range(NSTEEP):
        ai_p = _split4(A[hh] * i)
        mA_p = _split4(np.full(128, -A[hh]))
        for o in range(16):
            d128 = o - 12
            b0 = math.sqrt(128.0) * (sl[hh] * (128.0 * d128 - 256.0) - C[hh])
            b0_p = _split4(np.full(128, b0))
            c0 = (hh * 16 + o) * 128
            for r in range(4):
                blhs[r, c0:c0 + 128] = ai_p[r]
                blhs[4 + r, c0:c0 + 128] = mA_p[r]
                blhs[8 + r, c0:c0 + 128] = b0_p[r]

    ebias = np.zeros((128, (NH - NSTEEP) * 16), dtype=np.float32)
    for hh in range(NSTEEP, NH):
        for o in range(16):
            ebias[:, (hh - NSTEEP) * 16 + o] = (
                sl[hh] * (128.0 * (o - 12) + i - float(QREF))).astype(np.float32)

    return {
        "onescol": np.asarray(onescol),
        "lnones": np.asarray(lnones),
        "onesrow": np.asarray(onesrow),
        "trimask": np.asarray(tri),
        "biasrhs": np.asarray(brhs),
        "biaslhs": np.asarray(blhs),
        "expbias": ebias,
    }


_SCRATCH_KIND = "Internal"


def _dram_io(nc):
    sk = _SCRATCH_KIND
    H = {}
    H["xt"] = nc.dram_tensor("xt", [NB, D, T], F32, kind="ExternalInput").ap()
    for w in ("wq", "wk", "wv", "wo"):
        H[w] = nc.dram_tensor(w, [128, NCH, D], F32, kind="ExternalInput").ap()
    H["w1t"] = nc.dram_tensor("w1t", [NFC, 128, D], F32, kind="ExternalInput").ap()
    H["w2t"] = nc.dram_tensor("w2t", [4, 128, NFC * 512], F32, kind="ExternalInput").ap()
    H["onescol"] = nc.dram_tensor("onescol", [128, 1], BF16, kind="ExternalInput").ap()
    H["lnones"] = nc.dram_tensor("lnones", [128, 1], BF16, kind="ExternalInput").ap()
    H["onesrow"] = nc.dram_tensor("onesrow", [1, 128], BF16, kind="ExternalInput").ap()
    H["trimask"] = nc.dram_tensor("trimask", [128, 128], BF16, kind="ExternalInput").ap()
    H["biasrhs"] = nc.dram_tensor("biasrhs", [12, 512], BF16, kind="ExternalInput").ap()
    H["biaslhs"] = nc.dram_tensor("biaslhs", [12, NSTEEP * 16 * 128], BF16, kind="ExternalInput").ap()
    H["expbias"] = nc.dram_tensor("expbias", [128, (NH - NSTEEP) * 16], F32, kind="ExternalInput").ap()
    H["xn_h"] = nc.dram_tensor("xn_h", [NB, NCH, 128, T], BF16, kind=sk).ap()
    H["q_h"] = nc.dram_tensor("q_h", [NB, NCH, 128, T], BF16, kind=sk).ap()
    H["k_h"] = nc.dram_tensor("k_h", [NB, NCH, 128, T], BF16, kind=sk).ap()
    H["v_h"] = nc.dram_tensor("v_h", [NB, NCH, 128, T], BF16, kind=sk).ap()
    H["o_h"] = nc.dram_tensor("o_h", [NB, NCH, 128, T], BF16, kind=sk).ap()
    H["s_h"] = nc.dram_tensor("s_h", [NB, NCH, 128, T], BF16, kind=sk).ap()
    H["a_h"] = nc.dram_tensor("a_h", [NB * T // TGW, 128, NFC, TGW], BF16, kind=sk).ap()
    H["outt"] = nc.dram_tensor("outt", [NB, D, T], F32, kind="ExternalOutput").ap()
    return H


def phase_consts(nc, H, S):
    with contextlib.ExitStack() as _es:
        ld = _es.enter_context(nc.semaphore("c_ld"))
        blk = _es.enter_context(nc.Block("consts"))
        @blk.sync
        def _(sync):
            n = 0
            for k in ("onescol", "lnones", "onesrow"):
                sync.dma_start(S[k][:, :], H[k][:, :]).then_inc(ld, 16)
                n += 16
            sync.wait_ge(ld, n)

        @blk.vector
        def _(v):
            v.memset(S["eps"][:, :], 1e-5)


def phase_ln(nc, H, S):
    GN = NTG
    with contextlib.ExitStack() as _es:
        xg = _es.enter_context(nc.sbuf_tensor("ln_x", [128, 2 * NCH * TGW], F32))
        xbg = _es.enter_context(nc.sbuf_tensor("ln_xb", [128, NCH * TGW], BF16))
        x2g = _es.enter_context(nc.sbuf_tensor("ln_x2", [128, NCH * TGW], BF16))
        xng = _es.enter_context(nc.sbuf_tensor("ln_xn", [128, 2 * NCH * TGW], BF16))
        rows = _es.enter_context(nc.sbuf_tensor("ln_rows", [1, 6 * TGW], F32))
        rows16 = _es.enter_context(nc.sbuf_tensor("ln_rows16", [1, 2 * TGW], BF16))
        mps = _es.enter_context(nc.psum_tensor("ln_m", [1, TGW], F32))
        e2ps = _es.enter_context(nc.psum_tensor("ln_e2", [1, TGW], F32))
        Mps = _es.enter_context(nc.psum_tensor("ln_M", [128, TGW], F32))
        Rps = _es.enter_context(nc.psum_tensor("ln_R", [128, TGW], F32))
        xl0 = _es.enter_context(nc.semaphore("ln_xl0"))
        xl1 = _es.enter_context(nc.semaphore("ln_xl1"))
        vc = _es.enter_context(nc.semaphore("ln_vc"))
        ts = _es.enter_context(nc.semaphore("ln_ts"))
        st1 = _es.enter_context(nc.semaphore("ln_st1"))
        ss = _es.enter_context(nc.semaphore("ln_ss"))
        rp = _es.enter_context(nc.semaphore("ln_rp"))
        vd = _es.enter_context(nc.semaphore("ln_vd"))
        vp = _es.enter_context(nc.semaphore("ln_vp"))
        so0 = _es.enter_context(nc.semaphore("ln_so0"))
        so1 = _es.enter_context(nc.semaphore("ln_so1"))
        vcp = _es.enter_context(nc.semaphore("ln_vcp"))
        vsb = _es.enter_context(nc.semaphore("ln_vsb"))
        blk = _es.enter_context(nc.Block("ln"))
        @blk.sync
        def _(sync):
            for g in range(GN):
                b, t0 = g // 4, (g % 4) * TGW
                if g >= 2:
                    sync.wait_ge(vd, g - 1)
                half = (g % 2) * NCH * TGW
                for c0 in range(0, NCH, 4):
                    src = H["xt"][b][:, t0:t0 + TGW].rearrange("(c p) j -> p c j", p=128)[:, c0:c0 + 4, :]
                    dst = xg[:, half:half + NCH * TGW].rearrange("p (c j) -> p c j", c=NCH)[:, c0:c0 + 4, :]
                    sync.dma_start(dst, src).then_inc(xl0 if g % 2 == 0 else xl1, 16)

        @blk.vector
        def _(v):
            for g in range(GN):
                half = (g % 2) * NCH * TGW
                v.wait_ge(xl0 if g % 2 == 0 else xl1, 64 * (g // 2 + 1))
                if g >= 1:
                    v.wait_ge(vd, g)
                for c in range(NCH):
                    ins = v.tensor_copy(xbg[:, c * TGW:(c + 1) * TGW],
                                        xg[:, half + c * TGW: half + (c + 1) * TGW])
                    if c == NCH - 1:
                        ins.then_inc(vcp, 1)
                v.wait_ge(vcp, g + 1)
                for c in range(NCH):
                    ins = v.tensor_mul(x2g[:, c * TGW:(c + 1) * TGW],
                                       xbg[:, c * TGW:(c + 1) * TGW],
                                       xbg[:, c * TGW:(c + 1) * TGW])
                    if c == NCH - 1:
                        ins.then_inc(vc, 1)
                v.wait_ge(ts, g + 1)
                if g >= 1:
                    v.wait_ge(ss, g)
                v.tensor_copy(rows[:, 0:TGW], mps[0:1, :]).then_inc(vp, 1)
                v.tensor_copy(rows[:, TGW:2 * TGW], e2ps[0:1, :]).then_inc(vp, 1)
                v.wait_ge(vp, 4 * g + 2)
                v.tensor_mul(rows[:, 2 * TGW:3 * TGW], rows[:, 0:TGW], rows[:, 0:TGW]).then_inc(vp, 1)
                v.wait_ge(vp, 4 * g + 3)
                v.tensor_sub(rows[:, 3 * TGW:4 * TGW], rows[:, TGW:2 * TGW],
                             rows[:, 2 * TGW:3 * TGW]).then_inc(st1, 1)
                v.wait_ge(ss, g + 1)
                if g >= 1:
                    v.wait_ge(rp, g)
                v.reciprocal(rows[:, 5 * TGW:6 * TGW], rows[:, 4 * TGW:5 * TGW]).then_inc(vp, 1)
                v.wait_ge(vp, 4 * g + 4)
                v.tensor_copy(rows16[:, 0:TGW], rows[:, 0:TGW])
                v.tensor_copy(rows16[:, TGW:2 * TGW], rows[:, 5 * TGW:6 * TGW]).then_inc(st1, 1)
                v.wait_ge(rp, g + 1)
                if g >= 2:
                    v.wait_ge(so0 if g % 2 == 0 else so1, 64 * (g // 2))
                for c in range(NCH):
                    ins = v.tensor_sub(xg[:, half + c * TGW: half + (c + 1) * TGW],
                                       xg[:, half + c * TGW: half + (c + 1) * TGW], Mps[:, :])
                    if c == NCH - 1:
                        ins.then_inc(vsb, 1)
                v.wait_ge(vsb, g + 1)
                for c in range(NCH):
                    ins = v.tensor_mul(xng[:, half + c * TGW: half + (c + 1) * TGW],
                                       xg[:, half + c * TGW: half + (c + 1) * TGW], Rps[:, :])
                    if c == NCH - 1:
                        ins.then_inc(vd, 1)

        @blk.scalar
        def _(s):
            for g in range(GN):
                s.wait_ge(st1, 2 * g + 1)
                s.activation(rows[:, 4 * TGW:5 * TGW], rows[:, 3 * TGW:4 * TGW],
                             ACT.Sqrt, bias=S["eps"][0:1, 0:1], scale=1.0).then_inc(ss, 1)

        @blk.tensor
        def _(t):
            for g in range(GN):
                t.wait_ge(vc, g + 1)
                if g >= 1:
                    t.wait_ge(st1, 2 * (g - 1) + 1)
                for c in range(NCH):
                    t.matmul(mps[0:1, :], S["lnones"][:, 0:1],
                             xbg[:, c * TGW:(c + 1) * TGW], start=(c == 0), stop=(c == NCH - 1))
                for c in range(NCH):
                    ins = t.matmul(e2ps[0:1, :], S["lnones"][:, 0:1],
                                   x2g[:, c * TGW:(c + 1) * TGW], start=(c == 0), stop=(c == NCH - 1))
                    if c == NCH - 1:
                        ins.then_inc(ts, 1)
                t.wait_ge(st1, 2 * g + 2)
                if g >= 1:
                    t.wait_ge(vd, g)
                t.matmul(Mps[:, :], S["onesrow"][0:1, :], rows16[0:1, 0:TGW],
                         start=True, stop=True)
                t.matmul(Rps[:, :], S["onesrow"][0:1, :], rows16[0:1, TGW:2 * TGW],
                         start=True, stop=True).then_inc(rp, 1)

        @blk.gpsimd
        def _(gp):
            for g in range(GN):
                b, t0 = g // 4, (g % 4) * TGW
                gp.wait_ge(vd, g + 1)
                half = (g % 2) * NCH * TGW
                for c0 in range(0, NCH, 4):
                    dst = H["xn_h"][b][c0:c0 + 4, :, t0:t0 + TGW].rearrange("c p j -> p c j")
                    src = xng[:, half:half + NCH * TGW].rearrange("p (c j) -> p c j", c=NCH)[:, c0:c0 + 4, :]
                    gp.dma_start(dst, src).then_inc(so0 if g % 2 == 0 else so1, 16)
            gp.wait_ge(so0, 64 * GN // 2)
            gp.wait_ge(so1, 64 * GN // 2)


def phase_proj(nc, H, S, tag, w_name, in_name, out_name):
    GN = NTG
    with contextlib.ExitStack() as _es:
        wsb = _es.enter_context(nc.sbuf_tensor(tag + "_w", [128, NCH * D], BF16))
        ing = _es.enter_context(nc.sbuf_tensor(tag + "_in", [128, 2 * NCH * TGW], BF16))
        outg = _es.enter_context(nc.sbuf_tensor(tag + "_out", [128, 2 * NCH * TGW], BF16))
        p0 = _es.enter_context(nc.psum_tensor(tag + "_p0", [128, TGW], F32))
        p1 = _es.enter_context(nc.psum_tensor(tag + "_p1", [128, TGW], F32))
        wl = _es.enter_context(nc.semaphore(tag + "_wl"))
        il0 = _es.enter_context(nc.semaphore(tag + "_il0"))
        il1 = _es.enter_context(nc.semaphore(tag + "_il1"))
        mm = _es.enter_context(nc.semaphore(tag + "_mm"))
        eV = _es.enter_context(nc.semaphore(tag + "_eV"))
        eS = _es.enter_context(nc.semaphore(tag + "_eS"))
        so0 = _es.enter_context(nc.semaphore(tag + "_so0"))
        so1 = _es.enter_context(nc.semaphore(tag + "_so1"))
        blk = _es.enter_context(nc.Block(tag))
        psum = [p0, p1]

        @blk.gpsimd
        def _(gp):
            for c0 in range(0, NCH, 4):
                src = H[w_name][:, c0:c0 + 4, :]
                dst = wsb[:, :].rearrange("p (c o) -> p c o", c=NCH)[:, c0:c0 + 4, :]
                gp.dma_start(dst, src).then_inc(wl, 16)
            for g in range(GN):
                b, t0 = g // 4, (g % 4) * TGW
                gp.wait_ge(eV, 8 * (g + 1))
                gp.wait_ge(eS, 8 * (g + 1))
                half = (g % 2) * NCH * TGW
                for c0 in range(0, NCH, 4):
                    dst = H[out_name][b][c0:c0 + 4, :, t0:t0 + TGW].rearrange("c p j -> p c j")
                    src = outg[:, half:half + NCH * TGW].rearrange("p (c j) -> p c j", c=NCH)[:, c0:c0 + 4, :]
                    gp.dma_start(dst, src).then_inc(so0 if g % 2 == 0 else so1, 16)
            gp.wait_ge(so0, 64 * GN // 2)
            gp.wait_ge(so1, 64 * GN // 2)

        @blk.sync
        def _(sync):
            for g in range(GN):
                b, t0 = g // 4, (g % 4) * TGW
                if g >= 2:
                    sync.wait_ge(mm, 16 * (g - 1))
                half = (g % 2) * NCH * TGW
                for c0 in range(0, NCH, 4):
                    src = H[in_name][b][c0:c0 + 4, :, t0:t0 + TGW].rearrange("c p j -> p c j")
                    dst = ing[:, half:half + NCH * TGW].rearrange("p (c j) -> p c j", c=NCH)[:, c0:c0 + 4, :]
                    sync.dma_start(dst, src).then_inc(il0 if g % 2 == 0 else il1, 16)

        @blk.tensor
        def _(t):
            t.wait_ge(wl, 64)
            for g in range(GN):
                t.wait_ge(il0 if g % 2 == 0 else il1, 64 * (g // 2 + 1))
                half = (g % 2) * NCH * TGW
                for oc in range(NCH):
                    gi = g * NCH + oc
                    if gi >= 2:
                        t.wait_ge(eV if oc % 2 == 0 else eS, gi // 2)
                    ps = psum[oc % 2]
                    for c in range(NCH):
                        ins = t.matmul(ps[:, :],
                                       wsb[:, c * D + oc * 128: c * D + oc * 128 + 128],
                                       ing[:, half + c * TGW: half + (c + 1) * TGW],
                                       start=(c == 0), stop=(c == NCH - 1))
                        if c == NCH - 1:
                            ins.then_inc(mm, 1)

        @blk.vector
        def _(v):
            for g in range(GN):
                half = (g % 2) * NCH * TGW
                for oc in range(0, NCH, 2):
                    gi = g * NCH + oc
                    v.wait_ge(mm, gi + 1)
                    if g >= 2:
                        v.wait_ge(so0 if g % 2 == 0 else so1, 64 * (g // 2))
                    v.tensor_copy(outg[:, half + oc * TGW: half + (oc + 1) * TGW],
                                  psum[0][:, :]).then_inc(eV, 1)

        @blk.scalar
        def _(s):
            for g in range(GN):
                half = (g % 2) * NCH * TGW
                for oc in range(1, NCH, 2):
                    gi = g * NCH + oc
                    s.wait_ge(mm, gi + 1)
                    if g >= 2:
                        s.wait_ge(so0 if g % 2 == 0 else so1, 64 * (g // 2))
                    s.copy(outg[:, half + oc * TGW: half + (oc + 1) * TGW],
                           psum[1][:, :]).then_inc(eS, 1)


def phase_v(nc, H, S):
    GN = NTG
    with contextlib.ExitStack() as _es:
        wsb = _es.enter_context(nc.sbuf_tensor("v_w", [128, NCH * D], BF16))
        ing = _es.enter_context(nc.sbuf_tensor("v_in", [128, 2 * NCH * TGW], BF16))
        outg = _es.enter_context(nc.sbuf_tensor("v_out", [128, 2 * D], BF16))
        p0 = _es.enter_context(nc.psum_tensor("v_p0", [128, TGW], F32))
        p1 = _es.enter_context(nc.psum_tensor("v_p1", [128, TGW], F32))
        wl = _es.enter_context(nc.semaphore("v_wl"))
        il0 = _es.enter_context(nc.semaphore("v_il0"))
        il1 = _es.enter_context(nc.semaphore("v_il1"))
        mm = _es.enter_context(nc.semaphore("v_mm"))
        eV = _es.enter_context(nc.semaphore("v_eV"))
        eS = _es.enter_context(nc.semaphore("v_eS"))
        so0 = _es.enter_context(nc.semaphore("v_so0"))
        so1 = _es.enter_context(nc.semaphore("v_so1"))
        blk = _es.enter_context(nc.Block("vproj"))
        psum = [p0, p1]

        @blk.gpsimd
        def _(gp):
            for c0 in range(0, NCH, 4):
                src = H["wv"][:, c0:c0 + 4, :]
                dst = wsb[:, :].rearrange("p (c o) -> p c o", c=NCH)[:, c0:c0 + 4, :]
                gp.dma_start(dst, src).then_inc(wl, 16)
            for g in range(GN):
                b = g // 4
                for tt in range(4):
                    ti = g * 4 + tt
                    tc = (g % 4) * 4 + tt
                    gp.wait_ge(eV, 2 * (ti + 1))
                    gp.wait_ge(eS, 2 * (ti + 1))
                    src = outg[:, (ti % 2) * D:(ti % 2 + 1) * D]
                    gp.dma_start(H["v_h"][b][tc], src).then_inc(so0 if ti % 2 == 0 else so1, 16)
            gp.wait_ge(so0, 16 * 2 * GN)
            gp.wait_ge(so1, 16 * 2 * GN)

        @blk.sync
        def _(sync):
            for g in range(GN):
                b, t0 = g // 4, (g % 4) * TGW
                if g >= 2:
                    sync.wait_ge(mm, 16 * (g - 1))
                half = (g % 2) * NCH * TGW
                for c0 in range(0, NCH, 4):
                    src = H["xn_h"][b][c0:c0 + 4, :, t0:t0 + TGW].rearrange("c p j -> p c j")
                    dst = ing[:, half:half + NCH * TGW].rearrange("p (c j) -> p c j", c=NCH)[:, c0:c0 + 4, :]
                    sync.dma_start(dst, src).then_inc(il0 if g % 2 == 0 else il1, 16)

        @blk.tensor
        def _(t):
            t.wait_ge(wl, 64)
            for g in range(GN):
                t.wait_ge(il0 if g % 2 == 0 else il1, 64 * (g // 2 + 1))
                half = (g % 2) * NCH * TGW
                for tt in range(4):
                    for og in range(4):
                        ui = (g * 4 + tt) * 4 + og
                        if ui >= 2:
                            t.wait_ge(eV if ui % 2 == 0 else eS, ui // 2)
                        ps = psum[ui % 2]
                        for c in range(NCH):
                            ins = t.matmul(ps[:, :],
                                           ing[:, half + c * TGW + tt * 128: half + c * TGW + tt * 128 + 128],
                                           wsb[:, c * D + og * TGW: c * D + (og + 1) * TGW],
                                           start=(c == 0), stop=(c == NCH - 1))
                            if c == NCH - 1:
                                ins.then_inc(mm, 1)

        @blk.vector
        def _(v):
            for g in range(GN):
                for tt in range(4):
                    ti = g * 4 + tt
                    for og in range(4):
                        ui = ti * 4 + og
                        if ui % 2 != 0:
                            continue
                        v.wait_ge(mm, ui + 1)
                        if ti >= 2:
                            v.wait_ge(so0 if ti % 2 == 0 else so1, 16 * (ti // 2))
                        v.tensor_copy(outg[:, (ti % 2) * D + og * TGW:(ti % 2) * D + (og + 1) * TGW],
                                      psum[0][:, :]).then_inc(eV, 1)

        @blk.scalar
        def _(s):
            for g in range(GN):
                for tt in range(4):
                    ti = g * 4 + tt
                    for og in range(4):
                        ui = ti * 4 + og
                        if ui % 2 != 1:
                            continue
                        s.wait_ge(mm, ui + 1)
                        if ti >= 2:
                            s.wait_ge(so0 if ti % 2 == 0 else so1, 16 * (ti // 2))
                        s.copy(outg[:, (ti % 2) * D + og * TGW:(ti % 2) * D + (og + 1) * TGW],
                               psum[1][:, :]).then_inc(eS, 1)


def _attn_sched():
    out = []
    u = 0
    for qg in range(4):
        heads = []
        for hh in range(NH):
            nfull = 4 * qg
            lst = []
            ucol = 0
            if hh < NSTEEP:
                for p in range(0, nfull, 2):
                    lst.append((u, [p, p + 1], None, ucol)); u += 1; ucol += 1024
            else:
                for p in range(nfull):
                    lst.append((u, [p], None, ucol)); u += 1; ucol += 512
            for jj in range(4):
                lst.append((u, [nfull + jj], jj, ucol)); u += 1; ucol += 512
            heads.append(lst)
        out.append(heads)
    return out, u


def phase_attn(nc, H, S, b):
    tag = "at%d" % b
    SCHED, NU = _attn_sched()
    with contextlib.ExitStack() as _es:
        blbuf = _es.enter_context(nc.sbuf_tensor(tag + "_bl", [12, 2 * 16 * 128], BF16))
        brhs = _es.enter_context(nc.sbuf_tensor(tag + "_br", [12, 512], BF16))
        ebias = _es.enter_context(nc.sbuf_tensor(tag + "_eb", [128, (NH - NSTEEP) * 16], F32))
        tmask = _es.enter_context(nc.sbuf_tensor(tag + "_tm", [128, 128], BF16))
        ksb = _es.enter_context(nc.sbuf_tensor(tag + "_k", [128, NCH * T], BF16))
        vsb = _es.enter_context(nc.sbuf_tensor(tag + "_v", [128, NCH * T], BF16))
        qsb = _es.enter_context(nc.sbuf_tensor(tag + "_q", [128, NCH * TGW], BF16))
        psb = _es.enter_context(nc.sbuf_tensor(tag + "_p", [128, 16 * TGW], BF16))
        obuf = _es.enter_context(nc.sbuf_tensor(tag + "_ob", [128, 8 * TGW], BF16))
        sigrows = _es.enter_context(nc.sbuf_tensor(tag + "_sg", [4, TGW], F32))
        sig0 = _es.enter_context(nc.sbuf_tensor(tag + "_s0r", [1, 2 * 4 * TGW], F32))
        rr = _es.enter_context(nc.sbuf_tensor(tag + "_rr", [4, TGW], F32))
        rrb = _es.enter_context(nc.sbuf_tensor(tag + "_rb", [1, 4 * TGW], BF16))
        oq = _es.enter_context(nc.sbuf_tensor(tag + "_oq", [128, 2 * TGW], BF16))
        s0 = _es.enter_context(nc.psum_tensor(tag + "_s0", [128, 1024], F32))
        s1 = _es.enter_context(nc.psum_tensor(tag + "_s1", [128, 1024], F32))
        ops = _es.enter_context(nc.psum_tensor(tag + "_o", [128, TGW], F32))
        sigps = _es.enter_context(nc.psum_tensor(tag + "_sig", [1, TGW], F32))
        rps = _es.enter_context(nc.psum_tensor(tag + "_r", [128, TGW], F32))
        kvl = _es.enter_context(nc.semaphore(tag + "_kv"))
        bl0 = _es.enter_context(nc.semaphore(tag + "_bls0"))
        bl1 = _es.enter_context(nc.semaphore(tag + "_bls1"))
        ql0 = _es.enter_context(nc.semaphore(tag + "_ql0"))
        ql1 = _es.enter_context(nc.semaphore(tag + "_ql1"))
        sm = _es.enter_context(nc.semaphore(tag + "_sm"))
        ex = _es.enter_context(nc.semaphore(tag + "_ex"))
        mk = _es.enter_context(nc.semaphore(tag + "_mk"))
        pv = _es.enter_context(nc.semaphore(tag + "_pv"))
        oc = _es.enter_context(nc.semaphore(tag + "_oc"))
        nr = _es.enter_context(nc.semaphore(tag + "_nr"))
        nrd0 = _es.enter_context(nc.semaphore(tag + "_nrd0"))
        nrd1 = _es.enter_context(nc.semaphore(tag + "_nrd1"))
        sgd = _es.enter_context(nc.semaphore(tag + "_sgd"))
        rp = _es.enter_context(nc.semaphore(tag + "_rp"))
        nm = _es.enter_context(nc.semaphore(tag + "_nm"))
        so0 = _es.enter_context(nc.semaphore(tag + "_so0"))
        so1 = _es.enter_context(nc.semaphore(tag + "_so1"))
        blk = _es.enter_context(nc.Block(tag))
        spsum = [s0, s1]

        @blk.sync
        def _(sync):
            for k, dst in (("biasrhs", brhs),
                           ("expbias", ebias), ("trimask", tmask)):
                sync.dma_start(dst[:, :], H[k][:, :]).then_inc(kvl, 16)
            for c0 in range(0, NCH, 4):
                src = H["k_h"][b][c0:c0 + 4].rearrange("c p j -> p c j")
                dst = ksb[:, :].rearrange("p (c j) -> p c j", c=NCH)[:, c0:c0 + 4, :]
                sync.dma_start(dst, src).then_inc(kvl, 16)
            for c0 in range(0, NCH, 4):
                src = H["v_h"][b][c0:c0 + 4].rearrange("c p j -> p c j")
                dst = vsb[:, :].rearrange("p (c j) -> p c j", c=NCH)[:, c0:c0 + 4, :]
                sync.dma_start(dst, src).then_inc(kvl, 16)
            for qg in range(4):
                if qg >= 1:
                    sync.wait_ge(sm, SCHED[qg - 1][NH - 1][-1][0] + 1)
                t0 = qg * TGW
                for c0 in range(0, NCH, 4):
                    src = H["q_h"][b][c0:c0 + 4, :, t0:t0 + TGW].rearrange("c p j -> p c j")
                    dst = qsb[:, :].rearrange("p (c j) -> p c j", c=NCH)[:, c0:c0 + 4, :]
                    sync.dma_start(dst, src).then_inc(ql0 if qg % 2 == 0 else ql1, 16)
                for hh in range(NSTEEP):
                    li = qg * NSTEEP + hh
                    if li >= 2:
                        pqg, phh = divmod(li - 2, NSTEEP)
                        sync.wait_ge(sm, SCHED[pqg][phh][-1][0] + 1)
                    sync.dma_start(blbuf[:, (li % 2) * 2048:(li % 2 + 1) * 2048],
                                   H["biaslhs"][:, hh * 2048:(hh + 1) * 2048]).then_inc(bl0 if li % 2 == 0 else bl1, 16)

        @blk.tensor
        def _(t):
            def _emit_repl(t, qidx):
                t.wait_ge(nrd0 if qidx % 2 == 0 else nrd1, 64 * (qidx // 2 + 1))
                for hl in range(4):
                    gh = qidx * 4 + hl
                    if gh >= 1:
                        t.wait_ge(nm, gh)
                    t.matmul(rps[:, :], S["onesrow"][0:1, :],
                             rrb[0:1, hl * TGW:(hl + 1) * TGW],
                             start=True, stop=True).then_inc(rp, 1)
            t.wait_ge(kvl, 176)
            for qg in range(4):
                t.wait_ge(ql0 if qg % 2 == 0 else ql1, 64 * (qg // 2 + 1))
                for hh in range(NH):
                    units = SCHED[qg][hh]
                    head_idx = qg * NH + hh
                    nkc = 4 * qg + 4
                    cur_q = qg * 4 + hh // 4
                    if hh % 4 == 0 and cur_q >= 2:
                        _emit_repl(t, cur_q - 2)
                    if hh < NSTEEP:
                        li = qg * NSTEEP + hh
                        t.wait_ge(bl0 if li % 2 == 0 else bl1, 16 * (li // 2 + 1))
                    for (u, kcs, dj, ucol) in units:
                        slot = u % 2
                        if u >= 2:
                            t.wait_ge(ex, u - 1)
                        ins = None
                        for idx, kc in enumerate(kcs):
                            j0 = 0 if dj is None else dj * 128
                            width = TGW - j0
                            col0 = idx * 512 + j0
                            ins = t.matmul(
                                spsum[slot][:, col0:col0 + width],
                                ksb[:, hh * T + kc * 128: hh * T + kc * 128 + 128],
                                qsb[:, hh * TGW + j0: hh * TGW + TGW],
                                start=True, stop=(hh >= NSTEEP))
                            if hh < NSTEEP:
                                o = kc - 4 * qg + 12
                                ins = t.matmul(
                                    spsum[slot][:, col0:col0 + width],
                                    blbuf[:, ((qg * NSTEEP + hh) % 2) * 2048 + o * 128:
                                          ((qg * NSTEEP + hh) % 2) * 2048 + o * 128 + 128],
                                    brhs[:, j0:TGW],
                                    start=False, stop=True)
                        ins.then_inc(sm, 1)
                    if head_idx >= 1:
                        t.wait_ge(oc, 2 * head_idx)
                    first = True
                    ins = None
                    for (u, kcs, dj, ucol) in units:
                        if dj is None:
                            t.wait_ge(ex, u + 1)
                        else:
                            t.wait_ge(mk, head_idx * 4 + dj + 1)
                        for idx, kc in enumerate(kcs):
                            j0 = 0 if dj is None else dj * 128
                            width = TGW - j0
                            pcol = ucol + idx * 512 + j0
                            last = (kc == nkc - 1)
                            t.matmul(ops[:, j0:TGW],
                                     vsb[:, kc * T + hh * 128: kc * T + hh * 128 + 128],
                                     psb[:, pcol:pcol + width],
                                     start=first, stop=last)
                            ins = t.matmul(sigps[0:1, j0:TGW],
                                           S["onescol"][:, 0:1],
                                           psb[:, pcol:pcol + width],
                                           start=first, stop=last)
                            first = False
                    ins.then_inc(pv, 1)
                    if head_idx == NH * 4 - 1:
                        _emit_repl(t, 14)
                        _emit_repl(t, 15)

        @blk.scalar
        def _(s):
            for qg in range(4):
                for hh in range(NH):
                    for (u, kcs, dj, ucol) in SCHED[qg][hh]:
                        s.wait_ge(sm, u + 1)
                        slot = u % 2
                        j0 = 0 if dj is None else dj * 128
                        if len(kcs) == 2:
                            s.activation(psb[:, ucol:ucol + 1024],
                                         spsum[slot][:, 0:1024], ACT.Exp,
                                         bias=0.0, scale=INV_SQRT_HD).then_inc(ex, 1)
                        elif hh < NSTEEP:
                            s.activation(psb[:, ucol + j0:ucol + TGW],
                                         spsum[slot][:, j0:TGW], ACT.Exp,
                                         bias=0.0, scale=INV_SQRT_HD).then_inc(ex, 1)
                        else:
                            o = kcs[0] - 4 * qg + 12
                            bidx = (hh - NSTEEP) * 16 + o
                            s.activation(psb[:, ucol + j0:ucol + TGW],
                                         spsum[slot][:, j0:TGW], ACT.Exp,
                                         bias=ebias[:, bidx:bidx + 1],
                                         scale=INV_SQRT_HD).then_inc(ex, 1)

        @blk.vector
        def _(v):
            def emit_muls(q):
                for hl in range(4):
                    gh = q * 4 + hl
                    v.wait_ge(rp, gh + 1)
                    if gh >= 2:
                        v.wait_ge(so0 if gh % 2 == 0 else so1, 16 * (gh // 2))
                    v.tensor_mul(oq[:, (gh % 2) * TGW:(gh % 2 + 1) * TGW],
                                 obuf[:, ((q % 2) * 4 + hl) * TGW:((q % 2) * 4 + hl + 1) * TGW],
                                 rps[:, :]).then_inc(nm, 1)

            for qg in range(4):
                for hh in range(NH):
                    units = SCHED[qg][hh]
                    head_idx = qg * NH + hh
                    for (u, kcs, dj, ucol) in units:
                        if dj is not None:
                            v.wait_ge(ex, u + 1)
                            base = ucol + dj * 128
                            v.tensor_mul(psb[:, base:base + 128],
                                         psb[:, base:base + 128],
                                         tmask[:, :]).then_inc(mk, 1)
                    v.wait_ge(pv, head_idx + 1)
                    hl4 = hh % 4
                    qidx = qg * 4 + hh // 4
                    if hl4 == 0 and qidx >= 2:
                        v.wait_ge(sgd, 64 * (qidx - 1))
                    s0c = (qidx % 2) * 4 * TGW
                    v.tensor_copy(obuf[:, ((qidx % 2) * 4 + hl4) * TGW:
                                  ((qidx % 2) * 4 + hl4 + 1) * TGW], ops[:, :]).then_inc(oc, 1)
                    v.tensor_copy(sig0[0:1, s0c + hl4 * TGW: s0c + (hl4 + 1) * TGW],
                                  sigps[0:1, :]).then_inc(oc, 1)
                    if hh % 4 == 3:
                        if qidx >= 1:
                            v.wait_ge(nrd0 if (qidx - 1) % 2 == 0 else nrd1,
                                      64 * ((qidx - 1) // 2 + 1))
                        v.wait_ge(sgd, 64 * (qidx + 1))
                        v.reciprocal(rr[:, :], sigrows[:, :]).then_inc(nr, 1)
                        if qidx >= 1:
                            emit_muls(qidx - 1)
                        if qidx == 15:
                            emit_muls(15)

        @blk.gpsimd
        def _(gp):
            def stores(q):
                t0s = (q // 4) * TGW
                for hl in range(4):
                    gh = q * 4 + hl
                    hh = (q % 4) * 4 + hl
                    gp.wait_ge(nm, gh + 1)
                    gp.dma_start(H["o_h"][b][hh][:, t0s:t0s + TGW],
                                 oq[:, (gh % 2) * TGW:(gh % 2 + 1) * TGW]).then_inc(so0 if gh % 2 == 0 else so1, 16)

            for qg in range(4):
                for qb in range(4):
                    qidx = qg * 4 + qb
                    gp.wait_ge(oc, 2 * (4 * qidx + 4))
                    if qidx >= 1:
                        gp.wait_ge(nr, qidx)
                    s0c = (qidx % 2) * 4 * TGW
                    for hl in range(4):
                        gp.dma_start(sigrows[hl:hl + 1, :],
                                     sig0[0:1, s0c + hl * TGW: s0c + (hl + 1) * TGW]).then_inc(sgd, 16)
                    if qidx >= 1:
                        stores(qidx - 1)
                    gp.wait_ge(nr, qidx + 1)
                    if qidx >= 1:
                        gp.wait_ge(rp, 4 * qidx)
                    for hl in range(4):
                        gp.dma_start(rrb[0:1, hl * TGW:(hl + 1) * TGW],
                                     rr[hl:hl + 1, :]).then_inc(nrd0 if qidx % 2 == 0 else nrd1, 16)
            stores(15)
            gp.wait_ge(so0, 16 * 32)
            gp.wait_ge(so1, 16 * 32)


def phase_f1(nc, H, S):
    NT = NB * T // TGW
    with contextlib.ExitStack() as _es:
        xnf = _es.enter_context(nc.sbuf_tensor("f1_xn", [128, NCH * NB * T], BF16))
        wst = _es.enter_context(nc.sbuf_tensor("f1_w", [128, 2 * D], BF16))
        ysb = _es.enter_context(nc.sbuf_tensor("f1_y", [128, 2 * TGW], F32))
        q0sb = _es.enter_context(nc.sbuf_tensor("f1_q0", [128, 2 * TGW], I16))
        dsb = _es.enter_context(nc.sbuf_tensor("f1_d", [128, 2 * TGW], F32))
        tA = _es.enter_context(nc.sbuf_tensor("f1_ta", [128, 2 * TGW], F32))
        tB = _es.enter_context(nc.sbuf_tensor("f1_tb", [128, 2 * TGW], F32))
        absf = _es.enter_context(nc.sbuf_tensor("f1_a", [128, 2 * NB * T], BF16))
        p0 = _es.enter_context(nc.psum_tensor("f1_p0", [128, TGW], F32))
        p1 = _es.enter_context(nc.psum_tensor("f1_p1", [128, TGW], F32))
        xl = _es.enter_context(nc.semaphore("f1_xl"))
        wl0 = _es.enter_context(nc.semaphore("f1_wl0"))
        wl1 = _es.enter_context(nc.semaphore("f1_wl1"))
        mm = _es.enter_context(nc.semaphore("f1_mm"))
        ys = _es.enter_context(nc.semaphore("f1_ys"))
        vq = _es.enter_context(nc.semaphore("f1_vq"))
        vp = _es.enter_context(nc.semaphore("f1_vp"))
        ge = _es.enter_context(nc.semaphore("f1_ge"))
        so0 = _es.enter_context(nc.semaphore("f1_so0"))
        so1 = _es.enter_context(nc.semaphore("f1_so1"))
        blk = _es.enter_context(nc.Block("f1"))
        psum = [p0, p1]

        @blk.sync
        def _(sync):
            for bb in range(NB):
                for c0 in range(0, NCH, 4):
                    src = H["xn_h"][bb][c0:c0 + 4].rearrange("c p j -> p c j")
                    dst = xnf[:, :].rearrange("p (c j) -> p c j", c=NCH)[
                        :, c0:c0 + 4, bb * T:(bb + 1) * T]
                    sync.dma_start(dst, src).then_inc(xl, 16)

        @blk.gpsimd
        def _(gp):
            for fc in range(2):
                gp.dma_start(wst[:, (fc % 2) * D:(fc % 2 + 1) * D],
                             H["w1t"][fc]).then_inc(wl0 if fc % 2 == 0 else wl1, 16)
            for fc in range(NFC):
                if fc + 2 < NFC:
                    gp.wait_ge(mm, NT * (fc + 1))
                    gp.dma_start(wst[:, (fc % 2) * D:(fc % 2 + 1) * D],
                                 H["w1t"][fc + 2]).then_inc(wl0 if fc % 2 == 0 else wl1, 16)
                gp.wait_ge(ge, NT * (fc + 1))
                for g in range(NT):
                    gp.dma_start(H["a_h"][g, :, fc, :],
                                 absf[:, (fc % 2) * NB * T + g * TGW:
                                      (fc % 2) * NB * T + (g + 1) * TGW]).then_inc(so0 if fc % 2 == 0 else so1, 16)
            gp.wait_ge(so0, 16 * NT * NFC // 2)
            gp.wait_ge(so1, 16 * NT * NFC // 2)

        @blk.tensor
        def _(t):
            t.wait_ge(xl, 16 * 8)
            for fc in range(NFC):
                t.wait_ge(wl0 if fc % 2 == 0 else wl1, 16 * (fc // 2 + 1))
                for tg in range(NT):
                    ui = fc * NT + tg
                    if ui >= 2:
                        t.wait_ge(ys, ui - 1)
                    ps = psum[ui % 2]
                    for c in range(NCH):
                        ins = t.matmul(ps[:, :],
                                       wst[:, (fc % 2) * D + c * 128:(fc % 2) * D + (c + 1) * 128],
                                       xnf[:, c * NB * T + tg * TGW: c * NB * T + (tg + 1) * TGW],
                                       start=(c == 0), stop=(c == NCH - 1))
                        if c == NCH - 1:
                            ins.then_inc(mm, 1)

        @blk.scalar
        def _(s):
            NTOT = NFC * NT
            for ui in range(NTOT + 1):
                if ui < NTOT:
                    hp = ui % 2
                    fc = ui // NT
                    s.wait_ge(mm, ui + 1)
                    if ui >= 2:
                        s.wait_ge(vq, ui - 1)
                    s.activation(q0sb[:, hp * TGW:(hp + 1) * TGW], psum[hp][:, :],
                                 ACT.Copy, bias=0.0, scale=10.0)
                    s.activation(ysb[:, hp * TGW:(hp + 1) * TGW], psum[hp][:, :],
                                 ACT.Copy, bias=0.0, scale=10.0).then_inc(ys, 1)
                if ui >= 1:
                    uj = ui - 1
                    hp = uj % 2
                    fc = uj // NT
                    tg = uj % NT
                    s.wait_ge(vq, uj + 1)
                    if fc >= 2 and tg == 0:
                        s.wait_ge(so0 if fc % 2 == 0 else so1, 16 * NT * (fc // 2))
                    s.activation(absf[:, (fc % 2) * NB * T + tg * TGW:(fc % 2) * NB * T + (tg + 1) * TGW],
                                 tA[:, hp * TGW:(hp + 1) * TGW],
                                 ACT.Gelu_apprx_tanh, bias=0.0, scale=-0.1).then_inc(ge, 1)

        @blk.vector
        def _(v):
            NTOT = NFC * NT

            def emit(u, k):
                hp = u % 2
                ysl = ysb[:, hp * TGW:(hp + 1) * TGW]
                q0 = q0sb[:, hp * TGW:(hp + 1) * TGW]
                dd = dsb[:, hp * TGW:(hp + 1) * TGW]
                a_ = tA[:, hp * TGW:(hp + 1) * TGW]
                b_ = tB[:, hp * TGW:(hp + 1) * TGW]
                if k == 0:
                    return v.tensor_sub(dd, ysl, q0)
                if k == 1:
                    return v.scalar_tensor_tensor(a_, dd, 0.5, q0, ALU.is_ge, ALU.add)
                if k == 2:
                    return v.scalar_tensor_tensor(b_, dd, -0.5, a_, ALU.is_le, ALU.subtract)
                return v.tensor_scalar(a_, b_, 32768.0, -32767.0, ALU.min, ALU.max)

            for p_ in range(NTOT // 2):
                u0, u1 = 2 * p_, 2 * p_ + 1
                base = 6 * p_
                v.wait_ge(ys, u0 + 1)
                emit(u0, 0).then_inc(vp, 1)
                v.wait_ge(ys, u1 + 1)
                emit(u1, 0).then_inc(vp, 1)
                for k in range(1, 3):
                    if k == 1:
                        if u0 >= 2:
                            v.wait_ge(ge, u0 - 1)
                    v.wait_ge(vp, base + 2 * (k - 1) + 1)
                    emit(u0, k).then_inc(vp, 1)
                    if k == 1 and u1 >= 2:
                        v.wait_ge(ge, u1 - 1)
                    v.wait_ge(vp, base + 2 * (k - 1) + 2)
                    emit(u1, k).then_inc(vp, 1)
                v.wait_ge(vp, base + 5)
                emit(u0, 3).then_inc(vq, 1)
                v.wait_ge(vp, base + 6)
                emit(u1, 3).then_inc(vq, 1)


def phase_f2(nc, H, S):
    NT = NB * T // TGW
    with contextlib.ExitStack() as _es:
        wsb = _es.enter_context(nc.sbuf_tensor("f2_w", [128, 2 * NFC * TGW], BF16))
        asb = _es.enter_context(nc.sbuf_tensor("f2_a", [128, 16 * TGW], BF16))
        ssb = _es.enter_context(nc.sbuf_tensor("f2_s", [128, 2 * 4 * TGW], BF16))
        xsb = _es.enter_context(nc.sbuf_tensor("f2_x", [128, 2 * 4 * TGW], F32))
        gsb = _es.enter_context(nc.sbuf_tensor("f2_g", [128, 2 * TGW], BF16))
        sgsb = _es.enter_context(nc.sbuf_tensor("f2_sg", [128, 2 * TGW], F32))
        osb = _es.enter_context(nc.sbuf_tensor("f2_o", [128, 8 * TGW], F32))
        q0 = _es.enter_context(nc.psum_tensor("f2_p0", [128, TGW], F32))
        q1 = _es.enter_context(nc.psum_tensor("f2_p1", [128, TGW], F32))
        q2 = _es.enter_context(nc.psum_tensor("f2_p2", [128, TGW], F32))
        q3 = _es.enter_context(nc.psum_tensor("f2_p3", [128, TGW], F32))
        q4 = _es.enter_context(nc.psum_tensor("f2_p4", [128, TGW], F32))
        q5 = _es.enter_context(nc.psum_tensor("f2_p5", [128, TGW], F32))
        q6 = _es.enter_context(nc.psum_tensor("f2_p6", [128, TGW], F32))
        q7 = _es.enter_context(nc.psum_tensor("f2_p7", [128, TGW], F32))
        wl0 = _es.enter_context(nc.semaphore("f2_wl0"))
        wl1 = _es.enter_context(nc.semaphore("f2_wl1"))
        al0 = _es.enter_context(nc.semaphore("f2_al0"))
        al1 = _es.enter_context(nc.semaphore("f2_al1"))
        al2 = _es.enter_context(nc.semaphore("f2_al2"))
        al3 = _es.enter_context(nc.semaphore("f2_al3"))
        aq = _es.enter_context(nc.semaphore("f2_aq"))
        sx0 = _es.enter_context(nc.semaphore("f2_sx0"))
        sx1 = _es.enter_context(nc.semaphore("f2_sx1"))
        si = _es.enter_context(nc.semaphore("f2_si"))
        vp2 = _es.enter_context(nc.semaphore("f2_vp"))
        cb = _es.enter_context(nc.semaphore("f2_cb"))
        so0 = _es.enter_context(nc.semaphore("f2_so0"))
        so1 = _es.enter_context(nc.semaphore("f2_so1"))
        blk = _es.enter_context(nc.Block("f2"))
        psums = [q0, q1, q2, q3, q4, q5, q6, q7]
        HW = NFC * TGW // 2

        @blk.gpsimd
        def _(gp):
            WS = NFC * TGW
            for h in range(2):
                gp.dma_start(wsb[:, h * HW:h * HW + HW],
                             H["w2t"][0][:, h * HW:(h + 1) * HW]).then_inc(wl0, 16)
            for st in range(4):
                for g in range(NT):
                    if g == 0 and st + 1 < 4:
                        if st >= 1:
                            gp.wait_ge(aq, 256 * NT * st)
                        wofs = ((st + 1) % 2) * WS
                        for h in range(2):
                            gp.dma_start(wsb[:, wofs + h * HW:wofs + h * HW + HW],
                                         H["w2t"][st + 1][:, h * HW:(h + 1) * HW]).then_inc(wl0 if (st + 1) % 2 == 0 else wl1, 16)
                    bb, t0 = g // 4, (g % 4) * TGW
                    gi = st * NT + g
                    for ocl in range(4):
                        oi = gi * 4 + ocl
                        gp.wait_ge(cb, oi + 1)
                        d0 = st * 512 + ocl * 128
                        gp.dma_start(H["outt"][bb][d0:d0 + 128, t0:t0 + TGW],
                                     osb[:, ((gi % 2) * 4 + ocl) * TGW:((gi % 2) * 4 + ocl + 1) * TGW]).then_inc(so0 if gi % 2 == 0 else so1, 16)
            gp.wait_ge(so0, 16 * 2 * NT * 4)
            gp.wait_ge(so1, 16 * 2 * NT * 4)

        @blk.sync
        def _(sync):
            for st in range(4):
                for g in range(NT):
                    bb, t0 = g // 4, (g % 4) * TGW
                    gi = st * NT + g
                    if gi >= 2:
                        sync.wait_ge(cb, 4 * (gi - 1))
                    sh = (gi % 2) * 4 * TGW
                    sxp = sx0 if gi % 2 == 0 else sx1
                    src = H["s_h"][bb][4 * st:4 * st + 4, :, t0:t0 + TGW].rearrange("c p j -> p c j")
                    dst = ssb[:, sh:sh + 4 * TGW].rearrange("p (c j) -> p c j", c=4)
                    sync.dma_start(dst, src).then_inc(sxp, 16)
                    src = H["xt"][bb][st * 512:(st + 1) * 512, t0:t0 + TGW].rearrange("(c p) j -> p c j", p=128)
                    dst = xsb[:, sh:sh + 4 * TGW].rearrange("p (c j) -> p c j", c=4)
                    sync.dma_start(dst, src).then_inc(sxp, 16)
                    for f0 in range(0, NFC, 4):
                        qi = gi * 16 + f0 // 4
                        if qi >= 4:
                            sync.wait_ge(aq, 16 * (qi - 3))
                        src = H["a_h"][g, :, f0:f0 + 4, :]
                        dst = asb[:, (qi % 4) * 4 * TGW:(qi % 4 + 1) * 4 * TGW].rearrange(
                            "p (f j) -> p f j", f=4)
                        sync.dma_start(dst, src).then_inc([al0, al1, al2, al3][qi % 4], 16)

        @blk.tensor
        def _(t):
            for st in range(4):
                t.wait_ge(wl0 if st % 2 == 0 else wl1, 32 * (st // 2 + 1))
                for g in range(NT):
                    gi = st * NT + g
                    pb = (gi % 2) * 4
                    if gi >= 2:
                        t.wait_ge(si, 4 * (gi - 1))
                    for f0 in range(0, NFC, 4):
                        qi = gi * 16 + f0 // 4
                        t.wait_ge([al0, al1, al2, al3][qi % 4], 16 * (qi // 4 + 1))
                        for ff in range(4):
                            fc = f0 + ff
                            wofs = (st % 2) * (NFC * TGW)
                            for ocl in range(4):
                                ins = t.matmul(psums[pb + ocl][:, :],
                                               wsb[:, wofs + fc * TGW + ocl * 128: wofs + fc * TGW + ocl * 128 + 128],
                                               asb[:, (qi % 4) * 4 * TGW + ff * TGW:(qi % 4) * 4 * TGW + (ff + 1) * TGW],
                                               start=(fc == 0), stop=(fc == NFC - 1))
                                if ff == 3 and ocl == 3:
                                    ins.then_inc(aq, 16)

        @blk.scalar
        def _(s):
            for st in range(4):
                for g in range(NT):
                    gi = st * NT + g
                    pb = (gi % 2) * 4
                    s.wait_ge(aq, 256 * (gi + 1))
                    for ocl in range(4):
                        oi = gi * 4 + ocl
                        if oi >= 2:
                            s.wait_ge(cb, oi - 1)
                        s.activation(gsb[:, (oi % 2) * TGW:(oi % 2 + 1) * TGW],
                                     psums[pb + ocl][:, :], ACT.Silu,
                                     bias=0.0, scale=1.0).then_inc(si, 1)

        @blk.vector
        def _(v):
            for st in range(4):
                for g in range(NT):
                    gi = st * NT + g
                    sh = (gi % 2) * 4 * TGW
                    v.wait_ge(sx0 if gi % 2 == 0 else sx1, 32 * (gi // 2 + 1))
                    for ocl in range(4):
                        oi = gi * 4 + ocl
                        v.wait_ge(si, oi + 1)
                        if gi >= 2:
                            v.wait_ge(so0 if gi % 2 == 0 else so1, 16 * 4 * (gi // 2))
                        v.tensor_mul(sgsb[:, (oi % 2) * TGW:(oi % 2 + 1) * TGW],
                                     gsb[:, (oi % 2) * TGW:(oi % 2 + 1) * TGW],
                                     ssb[:, sh + ocl * TGW: sh + (ocl + 1) * TGW]).then_inc(vp2, 1)
                        v.wait_ge(vp2, oi + 1)
                        v.tensor_add(osb[:, ((gi % 2) * 4 + ocl) * TGW:((gi % 2) * 4 + ocl + 1) * TGW],
                                     sgsb[:, (oi % 2) * TGW:(oi % 2 + 1) * TGW],
                                     xsb[:, sh + ocl * TGW: sh + (ocl + 1) * TGW]).then_inc(cb, 1)


def _phase_reset(nc):
    nc.reset(previous_sbuf_base=nc.sbuf_base, previous_sbuf_top=nc.sbuf_top,
             previous_psum_base=nc.psum_base, previous_psum_top=nc.psum_top,
             previous_local_dram_base=nc.local_dram_base,
             previous_shared_dram_base=nc.shared_dram_base)


def build_nc(phases=("consts", "ln", "q", "k", "v", "attn", "s", "f1", "f2")):
    import contextlib
    nc = bass.Bass("TRN2", target_bir_lowering=False)
    H = _dram_io(nc)
    with contextlib.ExitStack() as stack:
        S = {}
        S["onescol"] = stack.enter_context(nc.sbuf_tensor("c_onescol", [128, 1], BF16))
        S["lnones"] = stack.enter_context(nc.sbuf_tensor("c_lnones", [128, 1], BF16))
        S["onesrow"] = stack.enter_context(nc.sbuf_tensor("c_onesrow", [1, 128], BF16))
        S["eps"] = stack.enter_context(nc.sbuf_tensor("c_eps", [1, 1], F32))
        first = True
        def sep():
            nonlocal first
            if not first:
                _phase_reset(nc)
            first = False
        if "consts" in phases:
            sep(); phase_consts(nc, H, S)
        if "ln" in phases:
            sep(); phase_ln(nc, H, S)
        if "q" in phases:
            sep(); phase_proj(nc, H, S, "pq", "wq", "xn_h", "q_h")
        if "k" in phases:
            sep(); phase_proj(nc, H, S, "pk", "wk", "xn_h", "k_h")
        if "v" in phases:
            sep(); phase_v(nc, H, S)
        if "attn" in phases:
            sep(); phase_attn(nc, H, S, 0)
            sep(); phase_attn(nc, H, S, 1)
        if "s" in phases:
            sep(); phase_proj(nc, H, S, "po", "wo", "o_h", "s_h")
        if "f1" in phases:
            sep(); phase_f1(nc, H, S)
        if "f2" in phases:
            sep(); phase_f2(nc, H, S)
    return nc


_CONSTS = None
_NC_CACHE = None


def _host_inputs(L0_attn_x, Wq, Wk, Wv, Wo, W1, W2):
    global _CONSTS
    if _CONSTS is None:
        _CONSTS = build_consts()
    C = _CONSTS
    base = {}
    for nm, W in (("wq", Wq), ("wk", Wk), ("wv", Wv), ("wo", Wo)):
        base[nm] = np.ascontiguousarray(
            np.asarray(W, dtype=np.float32).reshape(D, NCH, 128).transpose(2, 1, 0))
    base["w1t"] = np.ascontiguousarray(
        np.asarray(W1, dtype=np.float32).reshape(NFC, 128, NCH, 128).transpose(0, 3, 2, 1)).reshape(NFC, 128, D)
    base["w2t"] = np.ascontiguousarray(
        np.asarray(W2, dtype=np.float32).reshape(4, 512, NFC, 128).transpose(0, 3, 2, 1)).reshape(4, 128, NFC * 512)
    base.update({k: C[k] for k in C})
    x = np.asarray(L0_attn_x, dtype=np.float32)
    in_maps = []
    for c in range(8):
        m = dict(base)
        m["xt"] = np.ascontiguousarray(x[2 * c:2 * c + 2].transpose(0, 2, 1))
        in_maps.append(m)
    return in_maps


def kernel(L0_attn_x, L0_attn_Wq, L0_attn_Wk, L0_attn_Wv, L0_attn_Wo,
           L0_ffn_W1, L0_ffn_W2):
    global _NC_CACHE
    in_maps = _host_inputs(L0_attn_x, L0_attn_Wq, L0_attn_Wk, L0_attn_Wv,
                           L0_attn_Wo, L0_ffn_W1, L0_ffn_W2)
    if _NC_CACHE is None:
        _NC_CACHE = build_nc()
    res = run_bass_kernel_spmd(_NC_CACHE, in_maps, core_ids=list(range(8)))
    out = np.empty((16, T, D), dtype=np.float32)
    for c in range(8):
        r = np.asarray(res.results[c]["outt"])
        out[2 * c:2 * c + 2] = r.transpose(0, 2, 1)
    return out
